# version 02
# baseline (speedup 1.0000x reference)
import numpy as np
import ml_dtypes
from contextlib import ExitStack

import concourse.bass as bass
import concourse.mybir as mybir
from concourse.bass_utils import run_bass_kernel_spmd

F32 = mybir.dt.float32
BF16 = mybir.dt.bfloat16
AF = mybir.ActivationFunctionType
ALU = mybir.AluOpType
AX = mybir.AxisListType.X

D = 2048
FF = 1408
NE = 8
FS = 2816
NH = 16
DH = 128
SEQ = 4096
TLOC = 2048
TC = 1024
NKC = D // 128
ISD = 0.08838834764831845
NEG = -1.0e9
EPS = 1.0e-5

ABLK = [0, 3, 4, 7]
BBLK = [1, 2, 5, 6]
SLOTS = [1024, 2048, 3072, 4096]

ENGS = ("sync", "gpsimd", "tensor", "scalar", "vector")


class Sem:
    def __init__(self, nc, name):
        self.name = name
        self.h = nc.alloc_semaphore(name=name)
        self.n = 0


def _flat(waits):
    out = []
    for t in waits:
        if t is None:
            continue
        if isinstance(t, list):
            out.extend(x for x in t if x is not None)
        else:
            out.append(t)
    return out


class Bld:
    def __init__(self, nc):
        self.nc = nc
        self.sQ = Sem(nc, "sQ")
        self.sG = Sem(nc, "sG")
        self.sPE = Sem(nc, "sPE")
        self.sACT = Sem(nc, "sACT")
        self.sDVE = Sem(nc, "sDVE")
        self.sGP = Sem(nc, "sGP")
        self.sCC = [Sem(nc, "sCC0"), Sem(nc, "sCC1")]
        self.ops = None
        self.hw = None
        self.stage = None
        self.stage_tok = [None, None]
        self.stage_i = 0

    def begin(self):
        self.ops = {e: [] for e in ENGS}
        self.hw = {e: {} for e in ENGS}

    def op(self, eng, fn, waits=(), inc=None, amt=1):
        ws = []
        for sem, val in _flat(waits):
            if val > self.hw[eng].get(sem.name, 0):
                self.hw[eng][sem.name] = val
                ws.append((sem, val))
        tok = None
        if inc is not None:
            inc.n += amt
            tok = (inc, inc.n)
        self.ops[eng].append((fn, ws, (inc, amt) if inc is not None else None))
        return tok

    def flush(self, name=None):
        self.op("sync", None, waits=[(self.sQ, self.sQ.n)])
        self.op("gpsimd", None, waits=[(self.sG, self.sG.n)])
        nc = self.nc
        with nc.Block(name) as blk:
            for eng in ENGS:
                lst = self.ops[eng]
                if not lst:
                    continue

                def make(lst=lst):
                    def emit(e):
                        for fn, ws, inc in lst:
                            for sem, val in ws:
                                e.wait_ge(sem.h, val)
                            if fn is None:
                                continue
                            ins = fn(e)
                            if inc is not None:
                                ins.then_inc(inc[0].h, inc[1])
                    return emit
                getattr(blk, eng)(make())
        self.ops = None

    def qdma(self, out, in_, waits=(), transpose=False):
        if transpose:
            return self.op("sync", lambda e: e.dma_start(out, in_, transpose=True),
                           waits, self.sQ, 16)
        return self.op("sync", lambda e: e.dma_start(out, in_), waits, self.sQ, 16)

    def gdma(self, out, in_, waits=()):
        return self.op("gpsimd", lambda e: e.dma_start(out, in_), waits, self.sG, 16)

    def mm(self, out, lhsT, rhs, start, stop, waits=(), inc=False):
        return self.op("tensor",
                       lambda e: e.matmul(out, lhsT, rhs, start=start, stop=stop),
                       waits, self.sPE if inc else None)

    def tr(self, out, in_, ident, waits=()):
        return self.op("tensor", lambda e: e.transpose(out, in_, ident),
                       waits, self.sPE)

    def act(self, fn, out, in_, waits=(), **kw):
        return self.op("scalar", lambda e: e.activation(out, in_, fn, **kw),
                       waits, self.sACT)

    def dve(self, meth, *args, waits=(), **kw):
        return self.op("vector", lambda e: getattr(e, meth)(*args, **kw),
                       waits, self.sDVE)

    def gpc(self, meth, *args, waits=(), **kw):
        return self.op("gpsimd", lambda e: getattr(e, meth)(*args, **kw),
                       waits, self.sGP)

    def create_wt_cast(self, src_f32, nb):
        return self.gdma(nb[:, :], src_f32)

    def create_wt(self, ct, nb, wt, rows, cols):
        st = None
        for k in range(cols // 128):
            j = self.stage_i % 2
            lt = self.qdma(self.stage[j][:, 0:rows], nb[0:rows, k * 128:(k + 1) * 128],
                           waits=[ct, self.stage_tok[j]], transpose=True)
            st = self.gdma(wt[k * 128:(k + 1) * 128, 0:rows], self.stage[j][:, 0:rows],
                           waits=[lt])
            self.stage_tok[j] = st
            self.stage_i += 1
        return st


def _ap(t, offset, dims):
    return bass.AP(t, offset, [list(d) for d in dims])


def build_nc(dbg=(), phases=("f1", "attn", "f2")):
    nc = bass.Bass("TRN2", target_bir_lowering=False)

    xin = nc.dram_tensor("xin", [TLOC, D], F32, kind="ExternalInput")
    w_q = nc.dram_tensor("w_q", [NH, DH, D], F32, kind="ExternalInput")
    w_k = nc.dram_tensor("w_k", [NH, DH, D], F32, kind="ExternalInput")
    w_v = nc.dram_tensor("w_v", [NH, DH, D], F32, kind="ExternalInput")
    w_o = nc.dram_tensor("w_o", [D, D], F32, kind="ExternalInput")
    w_r = nc.dram_tensor("w_r", [NE, D], F32, kind="ExternalInput")
    w_g = nc.dram_tensor("w_g", [NE, FF, D], F32, kind="ExternalInput")
    w_u = nc.dram_tensor("w_u", [NE, FF, D], F32, kind="ExternalInput")
    w_d = nc.dram_tensor("w_d", [NE, D, FF], F32, kind="ExternalInput")
    w_sg = nc.dram_tensor("w_sg", [FS, D], F32, kind="ExternalInput")
    w_su = nc.dram_tensor("w_su", [FS, D], F32, kind="ExternalInput")
    w_sd = nc.dram_tensor("w_sd", [D, FS], F32, kind="ExternalInput")
    identf = nc.dram_tensor("identf", [128, 128], F32, kind="ExternalInput")
    identb = nc.dram_tensor("identb", [128, 128], BF16, kind="ExternalInput")
    ocol_b = nc.dram_tensor("ocol_b", [128, 1], BF16, kind="ExternalInput")
    orow_b = nc.dram_tensor("orow_b", [1, 128], BF16, kind="ExternalInput")
    ocol_f = nc.dram_tensor("ocol_f", [128, 1], F32, kind="ExternalInput")
    orow_f = nc.dram_tensor("orow_f", [1, 128], F32, kind="ExternalInput")
    dmask_in = nc.dram_tensor("dmask_in", [4, 128, 8 * 512], BF16,
                              kind="ExternalInput")

    out = nc.dram_tensor("outp", [TLOC, D], F32, kind="ExternalOutput")

    def idram(name, shape, dt):
        kind = "ExternalOutput" if name in dbg else "Internal"
        return nc.dram_tensor(name, shape, dt, kind=kind)

    nb_q = idram("nb_q", [D, D], BF16); wt_q = idram("wt_q", [D, D], BF16)
    nb_k = idram("nb_k", [D, D], BF16); wt_k = idram("wt_k", [D, D], BF16)
    nb_v = idram("nb_v", [D, D], BF16); wt_v = idram("wt_v", [D, D], BF16)
    nb_o = idram("nb_o", [D, D], BF16); wt_o = idram("wt_o", [D, D], BF16)
    nb_r = idram("nb_r", [NE, D], BF16)
    nb_g = idram("nb_g", [NE, FF, D], BF16); wt_g = idram("wt_g", [NE, D, FF], BF16)
    nb_u = idram("nb_u", [NE, FF, D], BF16); wt_u = idram("wt_u", [NE, D, FF], BF16)
    nb_d = idram("nb_d", [NE, D, FF], BF16); wt_d = idram("wt_d", [NE, FF, D], BF16)
    nb_sg = idram("nb_sg", [FS, D], BF16); wt_sg = idram("wt_sg", [D, FS], BF16)
    nb_su = idram("nb_su", [FS, D], BF16); wt_su = idram("wt_su", [D, FS], BF16)
    nb_sd = idram("nb_sd", [D, FS], BF16); wt_sd = idram("wt_sd", [FS, D], BF16)
    x_mid = idram("x_mid", [D, TLOC], F32)
    x2 = idram("x2", [D, TLOC], F32)
    qt = idram("qt", [D, TLOC], BF16)
    agin = [idram(f"agin{c}", [4112, TC], BF16) for c in range(2)]
    agout = [idram(f"agout{c}", [2 * 4112, TC], BF16) for c in range(2)]
    wbounce = idram("wbounce", [NE, TC], BF16)

    b = Bld(nc)
    wt_tok = {}

    def _job_desc(nm):
        if nm == "sg":
            return (w_sg[:, :], nb_sg, wt_sg, FS, D)
        if nm == "su":
            return (w_su[:, :], nb_su, wt_su, FS, D)
        if nm == "sd":
            return (w_sd[:, :], nb_sd, wt_sd, D, FS)
        if nm == "q":
            return (w_q.ap().rearrange("h p d -> (h p) d"), nb_q, wt_q, D, D)
        if nm == "k":
            return (w_k.ap().rearrange("h p d -> (h p) d"), nb_k, wt_k, D, D)
        if nm == "v":
            return (w_v.ap().rearrange("h p d -> (h p) d"), nb_v, wt_v, D, D)
        if nm == "o":
            return (w_o[:, :], nb_o, wt_o, D, D)
        kind, e = nm[0], int(nm[1])
        src = {"g": w_g, "u": w_u, "d": w_d}[kind]
        nb = {"g": nb_g, "u": nb_u, "d": nb_d}[kind]
        wt = {"g": wt_g, "u": wt_u, "d": wt_d}[kind]
        rows, cols = (D, FF) if kind == "d" else (FF, D)
        return (src[e, :, :], nb[e, :, :], wt[e, :, :], rows, cols)

    def creation_jobs(names):
        descs = [(nm, _job_desc(nm)) for nm in names]
        casts = {}
        for nm, (s, nb, wt, rows, cols) in descs:
            casts[nm] = b.create_wt_cast(s, nb)
        for nm, (s, nb, wt, rows, cols) in descs:
            wt_tok[nm] = b.create_wt(casts[nm], nb, wt, rows, cols)

    def kof_view(dst, n):
        return dst.ap().rearrange("p (kc f) -> p kc f", kc=n)

    def kof_src(wt2d, col0):
        return wt2d[:, col0:col0 + 128].rearrange("(kc p) f -> p kc f", p=128)

    with ExitStack() as top:
        ent = top.enter_context
        stage = [ent(nc.sbuf_tensor(f"stg{i}", [128, FS], BF16)) for i in range(2)]
        b.stage = stage
        c_idf = ent(nc.sbuf_tensor("c_idf", [128, 128], F32))
        c_idb = ent(nc.sbuf_tensor("c_idb", [128, 128], BF16))
        c_ocb = ent(nc.sbuf_tensor("c_ocb", [128, 1], BF16))
        c_orb = ent(nc.sbuf_tensor("c_orb", [1, 128], BF16))
        c_ocf = ent(nc.sbuf_tensor("c_ocf", [128, 1], F32))
        c_orf = ent(nc.sbuf_tensor("c_orf", [1, 128], F32))
        wrT = ent(nc.sbuf_tensor("wrT", [128, NKC * NE], BF16))

        b.begin()
        for dst, src in ((c_idf, identf), (c_idb, identb), (c_ocb, ocol_b),
                         (c_orb, orow_b), (c_ocf, ocol_f), (c_orf, orow_f)):
            b.qdma(dst[:, :], src[:, :])
        tcast = b.gdma(nb_r[:, :], w_r[:, :])
        with ExitStack() as bx:
            wr_sb = bx.enter_context(nc.sbuf_tensor("wr_sb", [NE, D], BF16))
            pb7 = bx.enter_context(nc.psum_tensor("pb7_0", [128, 1024], BF16))
            tl = b.qdma(wr_sb[:, :], nb_r[:, :], waits=[tcast])
            prev = None
            for k in range(NKC):
                tt = b.tr(pb7[:, 0:NE], wr_sb[0:NE, k * 128:(k + 1) * 128],
                          c_idb[0:NE, 0:NE], waits=[tl, prev])
                prev = b.dve("tensor_copy", wrT[:, k * NE:(k + 1) * NE],
                             pb7[:, 0:NE], waits=[tt])
            b.flush("b0")

        def ffn_block(c, first, src_fm, dst_fm, do_qkv, do_out, jobs):
            b.begin()
            war = {}

            def gate(key):
                return war.get(key, [])

            def setw(key, *toks):
                war[key] = [t for t in toks if t is not None]

            with ExitStack() as bx:
                tag = f"_{'f' if first else 's'}{c}"
                sbt = lambda nm, sh, dt: bx.enter_context(
                    nc.sbuf_tensor(nm + tag, sh, dt))
                pst = lambda nm, sh, dt: bx.enter_context(
                    nc.psum_tensor(nm + tag, sh, dt))
                acc = sbt("acc", [128, NKC * TC], F32)
                xn = sbt("xn", [128, NKC * TC], BF16)
                hs = sbt("hs", [128, 11 * TC], BF16)
                xsq = [sbt(f"xsq{i}", [128, 512], BF16) for i in range(2)]
                xm = [sbt(f"xm{i}", [128, 512], F32) for i in range(2)]
                mb = sbt("mb", [128, 512], F32)
                rb = sbt("rb", [128, 512], F32)
                kofg = [sbt(f"kofg{i}", [128, 2048], BF16) for i in range(2)]
                kofu = [sbt(f"kofu{i}", [128, 2048], BF16) for i in range(2)]
                kofd = [sbt(f"kofd{i}", [128, 11 * 128], BF16) for i in range(2)]
                wb = [sbt(f"wb{i}", [128, TC], BF16) for i in range(2)]
                sgt = [sbt(f"sgt{i}", [128, 512], BF16) for i in range(2)]
                h1 = [sbt(f"h1{i}", [128, 512], BF16) for i in range(2)]
                lg = sbt("lg", [128, 64], F32)
                m1 = sbt("m1", [128, 8], F32)
                m1n = sbt("m1n", [128, 8], F32)
                ee = sbt("ee", [128, 64], F32)
                es = sbt("es", [128, 8], F32)
                rs = sbt("rs", [128, 8], F32)
                prb = sbt("prb", [128, 64], F32)
                m1p = sbt("m1p", [128, 8], F32)
                msk = sbt("msk", [128, 64], F32)
                m2 = sbt("m2", [128, 8], F32)
                keep = sbt("keep", [128, 64], F32)
                ww = sbt("ww", [128, 64], F32)
                wtb = sbt("wtb", [128, 64], BF16)
                wtr = sbt("wtr", [NE, TC], BF16)
                wrow = sbt("wrow", [1, TC], BF16)
                rows = [sbt(f"row{i}", [1, 512], F32) for i in range(4)]
                rowb = [sbt(f"rowb{i}", [1, 512], BF16) for i in range(2)]
                xstage = [sbt("xstage0", [128, 2048], F32)] * 2
                if do_qkv:
                    kqv = [sbt(f"kqv{i}", [128, TC], BF16) for i in range(2)]
                    ksq = sbt("ksq", [128, TC], BF16)
                    knr = sbt("knr", [1, TC], BF16)
                    vtm = sbt("vtm", [128, TC], BF16)
                p0 = pst("p0", [128, 512], F32)
                p1 = pst("p1", [128, 512], F32)
                p2 = pst("p2", [128, 512], F32)
                p3 = pst("p3", [128, 512], F32)
                p4 = pst("p4", [128, 512], F32)
                p5 = pst("p5", [128, 512], F32)
                p6 = pst("p6", [128, 512], F32)
                p7 = pst("p7", [128, 1024], BF16)

                if first:
                    for tti in range(8):
                        j = 0
                        tl = b.qdma(xstage[j][:, :],
                                    xin[c * TC + tti * 128: c * TC + (tti + 1) * 128, :],
                                    waits=[gate(("xst", j))])
                        last = None
                        for k in range(NKC):
                            pj = (p5, p6)[k % 2]
                            ttr = b.tr(pj[:, 0:128],
                                       xstage[j][:, k * 128:(k + 1) * 128],
                                       c_idf[:, :], waits=[tl, gate(("tb", k % 2))])
                            tcp = b.dve("tensor_copy",
                                        acc[:, k * TC + tti * 128: k * TC + (tti + 1) * 128],
                                        pj[:, 0:128], waits=[ttr])
                            setw(("tb", k % 2), tcp)
                            last = tcp
                        setw(("xst", j), last)
                    tok_acc = last
                else:
                    tok_acc = b.qdma(
                        acc.ap().rearrange("p (kc t) -> p kc t", kc=NKC),
                        src_fm[:, c * TC:(c + 1) * TC]
                        .rearrange("(kc p) t -> p kc t", p=128))
                war["accready"] = [tok_acc]

                creation_jobs(jobs)

                def emit_ln():
                    last = None
                    for th in range(2):
                        o5 = th * 512
                        tsum = None
                        for k in range(NKC):
                            sl = acc[:, k * TC + o5: k * TC + o5 + 512]
                            tsum = b.mm(p0[0:1, :], c_ocf[:, 0:1], sl,
                                        start=(k == 0), stop=(k == NKC - 1),
                                        waits=[gate("accready"), gate("p0")],
                                        inc=(k == NKC - 1))
                        tsq = None
                        for k in range(NKC):
                            sl = acc[:, k * TC + o5: k * TC + o5 + 512]
                            ta = b.act(AF.Square, xsq[k % 2][:, :], sl,
                                       waits=[gate(("xsq", k % 2)), gate("accready")])
                            tsq = b.mm(p1[0:1, :], c_ocb[:, 0:1], xsq[k % 2][:, :],
                                       start=(k == 0), stop=(k == NKC - 1),
                                       waits=[ta, gate("p1")], inc=True)
                            setw(("xsq", k % 2), tsq)
                        t1 = b.act(AF.Copy, rows[0][0:1, :], p0[0:1, :], waits=[tsum])
                        setw("p0", t1)
                        t2 = b.act(AF.Copy, rows[1][0:1, :], p1[0:1, :], waits=[tsq])
                        setw("p1", t2)
                        t3 = b.dve("tensor_scalar", rows[2][0:1, :], rows[0][0:1, :],
                                   1.0 / D, None, ALU.mult, waits=[t1])
                        t4 = b.dve("tensor_tensor", rows[3][0:1, :], rows[2][0:1, :],
                                   rows[2][0:1, :], ALU.mult, waits=[t3])
                        t4 = b.dve("scalar_tensor_tensor", rows[3][0:1, :],
                                   rows[1][0:1, :], 1.0 / D, rows[3][0:1, :],
                                   ALU.mult, ALU.subtract, waits=[t2, t4])
                        t4 = b.dve("tensor_scalar", rows[3][0:1, :], rows[3][0:1, :],
                                   EPS, None, ALU.add, waits=[t4])
                        t5 = b.act(AF.Sqrt, rows[1][0:1, :], rows[3][0:1, :],
                                   waits=[t4])
                        t6 = b.dve("reciprocal", rows[3][0:1, :], rows[1][0:1, :],
                                   waits=[t5])
                        t7 = b.dve("tensor_copy", rowb[0][0:1, :], rows[2][0:1, :],
                                   waits=[t3])
                        t8 = b.dve("tensor_copy", rowb[1][0:1, :], rows[3][0:1, :],
                                   waits=[t6])
                        tmm = b.mm(p2[:, :], c_orb[0:1, :], rowb[0][0:1, :], True, True,
                                   waits=[t7, gate("p2")], inc=True)
                        tmb = b.act(AF.Copy, mb[:, :], p2[:, :], waits=[tmm, gate("mbf")])
                        setw("p2", tmb)
                        trm = b.mm(p3[:, :], c_orb[0:1, :], rowb[1][0:1, :], True, True,
                                   waits=[t8, gate("p3")], inc=True)
                        trb = b.act(AF.Copy, rb[:, :], p3[:, :], waits=[trm, gate("rbf")])
                        setw("p3", trb)
                        for k in range(NKC):
                            sl = acc[:, k * TC + o5: k * TC + o5 + 512]
                            j = k % 2
                            ts = b.dve("tensor_tensor", xm[j][:, :], sl,
                                       mb[:, :], ALU.subtract,
                                       waits=[tmb, gate(("xmf", j))])
                            tx = b.dve("tensor_tensor",
                                       xn[:, k * TC + o5: k * TC + o5 + 512],
                                       xm[j][:, :], rb[:, :],
                                       ALU.mult, waits=[trb])
                            setw(("xmf", j), tx)
                            last = tx
                        setw("mbf", last)
                        setw("rbf", last)
                    return last

                tok_xn = emit_ln()
                war["xnready"] = [tok_xn]

                for tti in range(8):
                    tlg = None
                    for k in range(NKC):
                        tlg = b.mm(p4[:, 0:8],
                                   xn[:, k * TC + tti * 128: k * TC + (tti + 1) * 128],
                                   wrT[:, k * NE:(k + 1) * NE],
                                   start=(k == 0), stop=(k == NKC - 1),
                                   waits=[gate("xnready"), gate("p4")],
                                   inc=(k == NKC - 1))
                    te = b.act(AF.Copy, lg[:, tti * 8:(tti + 1) * 8], p4[:, 0:8],
                               waits=[tlg])
                    setw("p4", te)
                tok_lg = war["p4"][0]
                tmx = None
                for tti in range(8):
                    tmx = b.dve("reduce_max", m1[:, tti:tti + 1],
                                lg[:, tti * 8:(tti + 1) * 8], AX, waits=[tok_lg])
                tneg = b.dve("tensor_scalar", m1n[:, :], m1[:, :], -1.0, None,
                             ALU.mult, waits=[tmx])
                texp = None
                for tti in range(8):
                    texp = b.act(AF.Exp, ee[:, tti * 8:(tti + 1) * 8],
                                 lg[:, tti * 8:(tti + 1) * 8],
                                 bias=m1n[:, tti:tti + 1], waits=[tneg])
                tes = None
                for tti in range(8):
                    tes = b.dve("reduce_sum", es[:, tti:tti + 1],
                                ee[:, tti * 8:(tti + 1) * 8], AX, waits=[texp])
                trs = b.dve("reciprocal", rs[:, :], es[:, :], waits=[tes])
                tpr = None
                for tti in range(8):
                    tpr = b.gpc("tensor_scalar", prb[:, tti * 8:(tti + 1) * 8],
                                ee[:, tti * 8:(tti + 1) * 8], rs[:, tti:tti + 1],
                                None, ALU.mult, waits=[trs])
                tm1p = []
                for tti in range(8):
                    tm1p.append(b.dve("reduce_max", m1p[:, tti:tti + 1],
                                      prb[:, tti * 8:(tti + 1) * 8], AX, waits=[tpr]))
                teq = None
                for tti in range(8):
                    teq = b.gpc("tensor_scalar", msk[:, tti * 8:(tti + 1) * 8],
                                prb[:, tti * 8:(tti + 1) * 8], m1p[:, tti:tti + 1],
                                None, ALU.is_equal, waits=[tm1p[-1]])
                tmsk = b.dve("scalar_tensor_tensor", msk[:, :], msk[:, :], NEG,
                             prb[:, :], ALU.mult, ALU.add, waits=[teq])
                tm2 = []
                for tti in range(8):
                    tm2.append(b.dve("reduce_max", m2[:, tti:tti + 1],
                                     msk[:, tti * 8:(tti + 1) * 8], AX, waits=[tmsk]))
                tke = None
                for tti in range(8):
                    tke = b.gpc("tensor_scalar", keep[:, tti * 8:(tti + 1) * 8],
                                prb[:, tti * 8:(tti + 1) * 8], m2[:, tti:tti + 1],
                                None, ALU.is_ge, waits=[tm2[-1]])
                tw = b.dve("tensor_tensor", ww[:, :], prb[:, :], keep[:, :],
                           ALU.mult, waits=[tke])
                tw = b.dve("tensor_scalar", wtb[:, :], ww[:, :], 0.5, None,
                           ALU.mult, waits=[tw])
                tprv = None
                for tti in range(8):
                    ttr = b.tr(p7[0:NE, 0:128], wtb[:, tti * 8:(tti + 1) * 8],
                               c_idb[:, :], waits=[tw, tprv])
                    tprv = b.dve("tensor_copy", wtr[0:NE, tti * 128:(tti + 1) * 128],
                                 p7[0:NE, 0:128], waits=[ttr])
                setw("p7", tprv)
                tok_wbounce = b.qdma(wbounce[:, :], wtr[:, :], waits=[tprv])

                def gate_up_down(srcs, wbt, tok_wb, half_scale):
                    gsrc, usrc, dsrc = srcs
                    hs_tok = [None, None]
                    tgm = tum = None
                    for fs in range(11):
                        jg = fs % 2
                        sa, wta = gsrc(fs)
                        tgl = b.qdma(kof_view(kofg[jg], NKC), sa,
                                     waits=[wta, gate(("kofg", jg))])
                        sb_, wtb_ = usrc(fs)
                        tul = b.qdma(kof_view(kofu[jg], NKC), sb_,
                                     waits=[wtb_, gate(("kofu", jg))])
                        for th in range(2):
                            o5 = th * 512
                            pg = (p0, p1)[th]
                            pu = (p2, p3)[th]
                            tgm = tum = None
                            for k in range(NKC):
                                rhs = xn[:, k * TC + o5: k * TC + o5 + 512]
                                tgm = b.mm(pg[:, :], kofg[jg][:, k * 128:(k + 1) * 128],
                                           rhs, start=(k == 0), stop=(k == NKC - 1),
                                           waits=[tgl, gate("xnready"),
                                                  gate("p0" if th == 0 else "p1")],
                                           inc=(k == NKC - 1))
                                tum = b.mm(pu[:, :], kofu[jg][:, k * 128:(k + 1) * 128],
                                           rhs, start=(k == 0), stop=(k == NKC - 1),
                                           waits=[tul,
                                                  gate("p2" if th == 0 else "p3")],
                                           inc=(k == NKC - 1))
                            js = th
                            tsl = b.act(AF.Silu, sgt[js][:, :], pg[:, :],
                                        waits=[tgm, gate(("sgt", js))])
                            setw("p0" if th == 0 else "p1", tsl)
                            hsl = hs[:, fs * TC + o5: fs * TC + o5 + 512]
                            if wbt is None:
                                th2 = b.dve("tensor_tensor", hsl, sgt[js][:, :],
                                            pu[:, :], ALU.mult, waits=[tsl, tum])
                            else:
                                th1 = b.dve("tensor_tensor", h1[js][:, :],
                                            sgt[js][:, :], pu[:, :], ALU.mult,
                                            waits=[tsl, tum, gate(("h1", js))])
                                th2 = b.dve("tensor_tensor", hsl, h1[js][:, :],
                                            wbt[:, o5:o5 + 512], ALU.mult,
                                            waits=[tok_wb])
                                setw(("h1", js), th2)
                            setw("p2" if th == 0 else "p3", th2)
                            setw(("sgt", js), th2)
                            hs_tok[th] = th2
                        setw(("kofg", jg), tgm)
                        setw(("kofu", jg), tum)
                    for dm in range(NKC):
                        jd = dm % 2
                        sd_, wtd_ = dsrc(dm)
                        tdl = b.qdma(kof_view(kofd[jd], 11), sd_,
                                     waits=[wtd_, gate(("kofd", jd))])
                        tdm = None
                        for th in range(2):
                            o5 = th * 512
                            pd = (p4, p5)[th]
                            tdm = None
                            for kf in range(11):
                                rhs = hs[:, kf * TC + o5: kf * TC + o5 + 512]
                                tdm = b.mm(pd[:, :], kofd[jd][:, kf * 128:(kf + 1) * 128],
                                           rhs, start=(kf == 0), stop=(kf == 10),
                                           waits=[tdl, hs_tok[th],
                                                  gate("p4" if th == 0 else "p5")],
                                           inc=(kf == 10))
                            asl = acc[:, dm * TC + o5: dm * TC + o5 + 512]
                            if half_scale:
                                tad = b.dve("scalar_tensor_tensor", asl, pd[:, :],
                                            0.5, asl, ALU.mult, ALU.add, waits=[tdm])
                            else:
                                tad = b.dve("tensor_tensor", asl, asl, pd[:, :],
                                            ALU.add, waits=[tdm])
                            setw("p4" if th == 0 else "p5", tad)
                            war["accw"] = [tad]
                        setw(("kofd", jd), tdm)

                for half in range(2):
                    gate_up_down(
                        (lambda fs, half=half: (kof_src(wt_sg, (half * 11 + fs) * 128),
                                                wt_tok.get("sg")),
                         lambda fs, half=half: (kof_src(wt_su, (half * 11 + fs) * 128),
                                                wt_tok.get("su")),
                         lambda dm, half=half: (
                             wt_sd[half * 1408:(half + 1) * 1408,
                                   dm * 128:(dm + 1) * 128]
                             .rearrange("(kc p) f -> p kc f", p=128),
                             wt_tok.get("sd"))),
                        None, None, True)

                for e in range(NE):
                    jw = e % 2
                    trl = b.qdma(wrow[0:1, :], wbounce[e:e + 1, :],
                                 waits=[tok_wbounce, gate("wrow")])
                    twb = None
                    for th in range(2):
                        o5 = th * 512
                        tbm = b.mm(p6[:, :], c_orb[0:1, :], wrow[0:1, o5:o5 + 512],
                                   True, True, waits=[trl, gate("p6")], inc=True)
                        twb = b.act(AF.Copy, wb[jw][:, o5:o5 + 512], p6[:, :],
                                    waits=[tbm, gate(("wb", jw))])
                        setw("p6", twb)
                    setw("wrow", twb)
                    setw(("wb", jw), twb)

                    gate_up_down(
                        (lambda fs, e=e: (wt_g[e, :, fs * 128:(fs + 1) * 128]
                                          .rearrange("(kc p) f -> p kc f", p=128),
                                          wt_tok.get(f"g{e}")),
                         lambda fs, e=e: (wt_u[e, :, fs * 128:(fs + 1) * 128]
                                          .rearrange("(kc p) f -> p kc f", p=128),
                                          wt_tok.get(f"u{e}")),
                         lambda dm, e=e: (wt_d[e, :, dm * 128:(dm + 1) * 128]
                                          .rearrange("(kc p) f -> p kc f", p=128),
                                          wt_tok.get(f"d{e}"))),
                        wb[jw], twb, False)
                    setw(("wb", jw), war["accw"][0])

                tok_accdone = war["accw"][0]
                war["accready"] = [tok_accdone]

                if dst_fm is not None:
                    b.qdma(dst_fm[:, c * TC:(c + 1) * TC]
                           .rearrange("(kc p) t -> p kc t", p=128),
                           acc.ap().rearrange("p (kc t) -> p kc t", kc=NKC),
                           waits=[tok_accdone])

                if do_out:
                    for tti in range(8):
                        j = 0
                        last = None
                        for k in range(NKC):
                            pj = (p0, p1)[k % 2]
                            ttr = b.tr(pj[:, 0:128],
                                       acc[:, k * TC + tti * 128: k * TC + (tti + 1) * 128],
                                       c_idf[:, :],
                                       waits=[tok_accdone,
                                              gate("p0" if k % 2 == 0 else "p1")])
                            tcp = b.dve("tensor_copy",
                                        xstage[j][:, k * 128:(k + 1) * 128],
                                        pj[:, 0:128], waits=[ttr, gate(("xst", j))])
                            setw("p0" if k % 2 == 0 else "p1", tcp)
                            last = tcp
                        tso = b.qdma(out[c * TC + tti * 128: c * TC + (tti + 1) * 128, :],
                                     xstage[j][:, :], waits=[last])
                        setw(("xst", j), tso)

                if do_qkv:
                    tok_xn2 = emit_ln()
                    war["xnready"] = [tok_xn2]
                    ctoks = []
                    for wtW, dest in ((wt_q, "q"), (wt_k, "k"), (wt_v, "v")):
                        for h in range(NH):
                            jg = h % 2
                            tkl = b.qdma(kof_view(kofg[jg], NKC),
                                         kof_src(wtW, h * 128),
                                         waits=[wt_tok.get(dest), gate(("kofg", jg))])
                            evt = tqm = None
                            for th in range(2):
                                o5 = th * 512
                                pq = (p0, p1)[th]
                                tqm = None
                                for k in range(NKC):
                                    rhs = xn[:, k * TC + o5: k * TC + o5 + 512]
                                    tqm = b.mm(pq[:, :],
                                               kofg[jg][:, k * 128:(k + 1) * 128],
                                               rhs, start=(k == 0),
                                               stop=(k == NKC - 1),
                                               waits=[tkl, gate("xnready"),
                                                      gate("p0" if th == 0 else "p1")],
                                               inc=(k == NKC - 1))
                                evt = b.act(AF.Copy, kqv[jg][:, o5:o5 + 512],
                                            pq[:, :],
                                            waits=[tqm, gate(("kqv", jg))])
                                setw("p0" if th == 0 else "p1", evt)
                            setw(("kofg", jg), tqm)
                            if dest == "q":
                                tq = b.qdma(qt[h * 128:(h + 1) * 128,
                                               c * TC:(c + 1) * TC],
                                            kqv[jg][:, :], waits=[evt])
                                setw(("kqv", jg), tq)
                            elif dest == "k":
                                tk = b.qdma(agin[c][h * 128:(h + 1) * 128, :],
                                            kqv[jg][:, :], waits=[evt])
                                ctoks.append(tk)
                                tsq2 = b.act(AF.Square, ksq[:, :], kqv[jg][:, :],
                                             waits=[evt, gate("ksq")])
                                setw(("kqv", jg), tk, tsq2)
                                tn = None
                                for th in range(2):
                                    o5 = th * 512
                                    tnm = b.mm(p4[0:1, :], c_ocb[:, 0:1],
                                               ksq[:, o5:o5 + 512], True, True,
                                               waits=[tsq2, gate("p4")], inc=True)
                                    tn = b.act(AF.Copy, knr[0:1, o5:o5 + 512],
                                               p4[0:1, :],
                                               waits=[tnm, gate("knr")])
                                    setw("p4", tn)
                                setw("ksq", tn)
                                tkn = b.qdma(agin[c][4096 + h:4097 + h, :],
                                             knr[0:1, :], waits=[tn])
                                setw("knr", tkn)
                                ctoks.append(tkn)
                            else:
                                tvs = None
                                for tti in range(8):
                                    ttr = b.tr(p7[:, 0:128],
                                               kqv[jg][:, tti * 128:(tti + 1) * 128],
                                               c_idb[:, :], waits=[evt, gate("p7")])
                                    tvs = b.dve("tensor_copy",
                                                vtm[:, tti * 128:(tti + 1) * 128],
                                                p7[:, 0:128],
                                                waits=[ttr, gate("vtm")])
                                    setw("p7", tvs)
                                setw(("kqv", jg), tvs)
                                tvst = b.qdma(agin[c][2048 + h * 128:
                                                      2048 + (h + 1) * 128, :],
                                              vtm[:, :], waits=[tvs])
                                setw("vtm", tvst)
                                ctoks.append(tvst)
                    b.op("gpsimd",
                         lambda e, c=c: e.collective_compute(
                             "AllGather", ALU.bypass,
                             replica_groups=[[0, 1], [2, 3], [4, 5], [6, 7]],
                             ins=[agin[c].ap().opt()],
                             outs=[agout[c].ap().opt()]),
                         waits=[(b.sQ, max(t[1] for t in ctoks))],
                         inc=b.sCC[c], amt=1)

            b.flush(f"ffn{tag}")

        def attn_block():
            b.begin()
            war = {}

            def gate(key):
                return war.get(key, [])

            def setw(key, *toks):
                war[key] = [t for t in toks if t is not None]

            with ExitStack() as bx:
                sbt = lambda nm, sh, dt: bx.enter_context(
                    nc.sbuf_tensor("a_" + nm, sh, dt))
                pst = lambda nm, sh, dt: bx.enter_context(
                    nc.psum_tensor("a_" + nm, sh, dt))
                qsb = sbt("qsb", [128, NH * 512], BF16)
                ktt = [sbt(f"ktt{i}", [128, SEQ], BF16) for i in range(2)]
                vtt = [sbt(f"vtt{i}", [128, SEQ], BF16) for i in range(2)]
                at = [sbt(f"at{i}", [128, 512], BF16) for i in range(2)]
                tmpS = [sbt(f"tmpS{i}", [128, 512], F32) for i in range(2)]
                dmsk = sbt("dmsk", [128, 8 * 512], BF16)
                zst = sbt("zst", [128, NH * 512], BF16)
                xms = sbt("xms", [128, NKC * 512], F32)
                rbs = [sbt(f"rbs{i}", [128, 512], F32) for i in range(2)]
                kofo = [sbt(f"kofo{i}", [128, 2048], BF16) for i in range(2)]
                qsq = [sbt(f"qsq{i}", [128, 512], BF16) for i in range(2)]
                nmca = sbt("nmca", [128, 16], F32)
                knb = sbt("knb", [16, TC], BF16)
                kmc = sbt("kmc", [16, 2], F32)
                kmx = sbt("kmx", [1, 32], F32)
                qm2 = sbt("qm2", [1, 16], F32)
                m2v = sbt("m2v", [1, 16], F32)
                msq = sbt("msq", [1, 16], F32)
                ngm = sbt("ngm", [1, 16], BF16)
                rcb = [sbt(f"rcb{i}", [1, 512], F32) for i in range(2)]
                p0 = pst("p0", [128, 512], F32)
                p1 = pst("p1", [128, 512], F32)
                p2 = pst("p2", [128, 512], F32)
                p3 = pst("p3", [128, 512], F32)
                p4 = pst("p4", [128, 512], F32)
                p5 = pst("p5", [128, 512], F32)
                p6 = pst("p6", [128, 512], F32)

                def gmap(g):
                    cAG = 0 if g < 4 else 1
                    r, c0 = {0: (0, 0), 3: (0, 512), 1: (1, 0), 2: (1, 512),
                             4: (0, 0), 7: (0, 512), 5: (1, 0), 6: (1, 512)}[g]
                    return cAG, r, c0

                tok_kmx = []
                for si, cAG in enumerate((0, 1)):
                    tmm = None
                    for r in range(2):
                        tl = b.qdma(knb[:, :],
                                    agout[cAG][r * 4112 + 4096: r * 4112 + 4112, :],
                                    waits=[(b.sCC[cAG], 1), gate("knb")])
                        tm_ = b.dve("reduce_max", kmc[:, r:r + 1], knb[:, :], AX,
                                    waits=[tl])
                        setw("knb", tm_)
                        tmm = tm_
                    tmm = b.dve("tensor_tensor", kmc[:, 0:1], kmc[:, 0:1],
                                kmc[:, 1:2], ALU.max, waits=[tmm])
                    ttr = b.tr(p6[0:1, 0:16], kmc[0:16, 0:1], c_idf[0:16, 0:16],
                               waits=[tmm, gate("p6")])
                    tcp = b.dve("tensor_copy", kmx[0:1, si * 16:(si + 1) * 16],
                                p6[0:1, 0:16], waits=[ttr])
                    setw("p6", tcp)
                    if si == 1:
                        tcp = b.dve("tensor_tensor", kmx[0:1, 16:32],
                                    kmx[0:1, 16:32], kmx[0:1, 0:16], ALU.max,
                                    waits=[tcp])
                    tok_kmx.append(tcp)

                for qc in range(4):
                    slot = SLOTS[qc]
                    nt = slot // 128
                    need = [(b.sCC[0], 1)] + ([(b.sCC[1], 1)] if slot > 2048 else [])
                    kslice = kmx[0:1, 0:16] if slot <= 2048 else kmx[0:1, 16:32]
                    tok_km = tok_kmx[0] if slot <= 2048 else tok_kmx[1]

                    tql = b.qdma(qsb.ap().rearrange("p (h t) -> p h t", h=NH),
                                 qt[:, qc * 512:(qc + 1) * 512]
                                 .rearrange("(h p) t -> p h t", p=128),
                                 waits=[gate("qsb")])
                    tdm = b.qdma(dmsk[:, :], dmask_in[qc, :, :],
                                 waits=[gate("dmsk")])
                    txl = b.qdma(xms.ap().rearrange("p (kc t) -> p kc t", kc=NKC),
                                 x_mid[:, qc * 512:(qc + 1) * 512]
                                 .rearrange("(kc p) t -> p kc t", p=128),
                                 waits=[gate("xms")])

                    tqmx = None
                    for h in range(NH):
                        j = h % 2
                        tq2 = b.act(AF.Square, qsq[j][:, :],
                                    qsb[:, h * 512:(h + 1) * 512],
                                    waits=[tql, gate(("qsq", j))])
                        tqn = b.mm(p5[0:1, :], c_ocb[:, 0:1], qsq[j][:, :],
                                   True, True, waits=[tq2, gate("p5")], inc=True)
                        setw(("qsq", j), tqn)
                        tqmx = b.dve("reduce_max", qm2[0:1, h:h + 1], p5[0:1, :],
                                     AX, waits=[tqn])
                        setw("p5", tqmx)
                    tv = b.dve("tensor_tensor", m2v[0:1, :], qm2[0:1, :], kslice,
                               ALU.mult, waits=[tok_km, tqmx])
                    tv = b.act(AF.Sqrt, msq[0:1, :], m2v[0:1, :], waits=[tv])
                    tok_ngm = b.dve("tensor_scalar", ngm[0:1, :], msq[0:1, :],
                                    -ISD, None, ALU.mult, waits=[tv])

                    tnm = b.mm(p6[:, 0:16], c_orb[0:1, :], ngm[0:1, :],
                               True, True, waits=[tok_ngm, gate("p6")], inc=True)
                    tnc = b.dve("tensor_copy", nmca[:, :], p6[:, 0:16],
                                waits=[tnm, gate("nmca")])
                    setw("p6", tnc)
                    for h in range(NH):
                        j = h % 2
                        ktoks, vtoks = [], []
                        for g in range(slot // 512):
                            cAG, r, c0 = gmap(g)
                            tk = b.qdma(ktt[j][:, g * 512:(g + 1) * 512],
                                        agout[cAG][r * 4112 + h * 128:
                                                   r * 4112 + (h + 1) * 128,
                                                   c0:c0 + 512],
                                        waits=[gate(("ktt", j))] + need)
                            ktoks.append(tk)
                            tvv = b.qdma(vtt[j][:, g * 512:(g + 1) * 512],
                                         agout[cAG][r * 4112 + 2048 + h * 128:
                                                    r * 4112 + 2048 + (h + 1) * 128,
                                                    c0:c0 + 512],
                                         waits=[gate(("vtt", j))] + need)
                            vtoks.append(tvv)
                        tdn = tav = None
                        for ti in range(nt):
                            pS = (p0, p1)[ti % 2]
                            bs = "p0" if ti % 2 == 0 else "p1"
                            tS = b.mm(pS[:, :], ktt[j][:, ti * 128:(ti + 1) * 128],
                                      qsb[:, h * 512:(h + 1) * 512], True, True,
                                      waits=[ktoks[ti // 4], tql, gate(bs)],
                                      inc=True)
                            ja = ti % 2
                            if ti >= nt - 8:
                                j8 = ti - (nt - 8)
                                tms = b.dve("tensor_tensor", tmpS[ja][:, :],
                                            pS[:, :],
                                            dmsk[:, j8 * 512:(j8 + 1) * 512],
                                            ALU.add,
                                            waits=[tS, tdm, gate(("tmpS", ja))])
                                setw(bs, tms)
                                tex = b.act(AF.Exp, at[ja][:, :], tmpS[ja][:, :],
                                            bias=nmca[:, h:h + 1], scale=ISD,
                                            waits=[tms, tnc, gate(("at", ja))])
                                setw(("tmpS", ja), tex)
                            else:
                                tex = b.act(AF.Exp, at[ja][:, :], pS[:, :],
                                            bias=nmca[:, h:h + 1], scale=ISD,
                                            waits=[tS, tnc, gate(("at", ja))])
                                setw(bs, tex)
                            tdn = b.mm(p3[0:1, :], c_ocb[:, 0:1], at[ja][:, :],
                                       start=(ti == 0), stop=(ti == nt - 1),
                                       waits=[tex, gate("p3")], inc=True)
                            pzb = (p2, p5)[h % 2]
                            pzk = "p2" if h % 2 == 0 else "p5"
                            tav = b.mm(pzb[:, :], vtt[j][:, ti * 128:(ti + 1) * 128],
                                       at[ja][:, :], start=(ti == 0),
                                       stop=(ti == nt - 1),
                                       waits=[vtoks[ti // 4], gate(pzk)],
                                       inc=True)
                            setw(("at", ja), tdn, tav)
                        setw(("ktt", j), tav)
                        setw(("vtt", j), tav)
                        war.setdefault("nmca", []).append(tav)
                        jr = h % 2
                        trc = b.dve("reciprocal", rcb[jr][0:1, :], p3[0:1, :],
                                    waits=[tdn, gate(("rcb", jr))])
                        setw("p3", trc)
                        tbc = b.mm(p4[:, :], c_orf[0:1, :], rcb[jr][0:1, :],
                                   True, True, waits=[trc, gate("p4")], inc=True)
                        tre = b.act(AF.Copy, rbs[jr][:, :], p4[:, :],
                                    waits=[tbc, gate(("rbs", jr))])
                        setw("p4", tre)
                        setw(("rcb", jr), tre)
                        tz = b.dve("tensor_tensor", zst[:, h * 512:(h + 1) * 512],
                                   pzb[:, :], rbs[jr][:, :], ALU.mult,
                                   waits=[tre, tav, gate(("zs", h))])
                        setw(pzk, tz)
                        setw(("rbs", jr), tz)
                        war["zdone"] = [tz]
                    for dm in range(NKC):
                        jo = dm % 2
                        bo = "p5" if dm % 2 == 0 else "p6"
                        tol = b.qdma(kof_view(kofo[jo], NKC), kof_src(wt_o, dm * 128),
                                     waits=[wt_tok.get("o"), gate(("kofo", jo))])
                        two = None
                        for fk in range(NKC):
                            two = b.mm((p5, p6)[dm % 2][:, :],
                                       kofo[jo][:, fk * 128:(fk + 1) * 128],
                                       zst[:, fk * 512:(fk + 1) * 512],
                                       start=(fk == 0), stop=(fk == NKC - 1),
                                       waits=[tol, gate("zdone"), gate(bo)],
                                       inc=(fk == NKC - 1))
                        setw(("kofo", jo), two)
                        xsl = xms[:, dm * 512:(dm + 1) * 512]
                        tadd = b.dve("tensor_tensor", xsl, xsl,
                                     (p5, p6)[dm % 2][:, :], ALU.add,
                                     waits=[two, txl])
                        setw(bo, tadd)
                        tso = b.qdma(x2[dm * 128:(dm + 1) * 128,
                                        qc * 512:(qc + 1) * 512], xsl,
                                     waits=[tadd])
                        setw("xms", tso)
                    setw("qsb", tav)
                    setw("dmsk", tav)
                    for h in range(NH):
                        setw(("zs", h), two)

            b.flush("attn")

        do_kv = "attn" in phases
        ffn_block(0, True, None, x_mid, do_kv, False,
                  ["sg", "su", "sd"] + [f"{k}{e}" for e in range(NE) for k in "gud"]
                  + (["q", "k", "v"] if do_kv else []))
        ffn_block(1, True, None, x_mid, do_kv, False, ["o"] if do_kv else [])
        if "attn" in phases:
            attn_block()
        if "f2" in phases:
            ffn_block(0, False, x2, None, False, True, [])
            ffn_block(1, False, x2, None, False, True, [])

        b.begin()
        b.op("gpsimd", lambda e: e.nop(), waits=[(b.sQ, b.sQ.n), (b.sG, b.sG.n)])
        b.flush("fin")

    return nc



def _host_consts():
    return {
        "identf": np.eye(128, dtype=np.float32),
        "identb": np.eye(128, dtype=ml_dtypes.bfloat16),
        "ocol_b": np.ones((128, 1), dtype=ml_dtypes.bfloat16),
        "orow_b": np.ones((1, 128), dtype=ml_dtypes.bfloat16),
        "ocol_f": np.ones((128, 1), dtype=np.float32),
        "orow_f": np.ones((1, 128), dtype=np.float32),
    }


def _dmask_for(blocks):
    m = np.zeros((4, 128, 8, 512), dtype=np.float32)
    for qc in range(4):
        slot = SLOTS[qc]
        q0 = 512 * blocks[qc]
        for jj in range(8):
            ti = slot // 128 - 8 + jj
            kvpos = 128 * ti + np.arange(128)[:, None]
            qpos = q0 + np.arange(512)[None, :]
            m[qc, :, jj, :] = np.where(qpos >= kvpos, 0.0, NEG)
    return np.ascontiguousarray(m.reshape(4, 128, 8 * 512).astype(ml_dtypes.bfloat16))


def kernel(L0_attn_x, L0_attn_W_K, L0_attn_W_Q, L0_attn_W_V, L0_attn_W_O,
           L0_ffn_Wr, L0_ffn_Wg, L0_ffn_Wu, L0_ffn_Wd, L0_ffn_Wsg, L0_ffn_Wsu,
           L0_ffn_Wsd):
    x = np.ascontiguousarray(np.asarray(L0_attn_x, dtype=np.float32))
    ws = {
        "w_q": np.ascontiguousarray(np.asarray(L0_attn_W_Q, dtype=np.float32)),
        "w_k": np.ascontiguousarray(np.asarray(L0_attn_W_K, dtype=np.float32)),
        "w_v": np.ascontiguousarray(np.asarray(L0_attn_W_V, dtype=np.float32)),
        "w_o": np.ascontiguousarray(np.asarray(L0_attn_W_O, dtype=np.float32)),
        "w_r": np.ascontiguousarray(np.asarray(L0_ffn_Wr, dtype=np.float32)),
        "w_g": np.ascontiguousarray(np.asarray(L0_ffn_Wg, dtype=np.float32)),
        "w_u": np.ascontiguousarray(np.asarray(L0_ffn_Wu, dtype=np.float32)),
        "w_d": np.ascontiguousarray(np.asarray(L0_ffn_Wd, dtype=np.float32)),
        "w_sg": np.ascontiguousarray(np.asarray(L0_ffn_Wsg, dtype=np.float32)),
        "w_su": np.ascontiguousarray(np.asarray(L0_ffn_Wsu, dtype=np.float32)),
        "w_sd": np.ascontiguousarray(np.asarray(L0_ffn_Wsd, dtype=np.float32)),
    }
    consts = _host_consts()
    dmA = _dmask_for(ABLK)
    dmB = _dmask_for(BBLK)

    in_maps = []
    for core in range(8):
        bseq = core // 2
        blocks = ABLK if core % 2 == 0 else BBLK
        xs = np.concatenate([x[bseq, 512 * g:512 * (g + 1), :] for g in blocks],
                            axis=0)
        m = {"xin": np.ascontiguousarray(xs),
             "dmask_in": dmA if core % 2 == 0 else dmB}
        m.update(ws)
        m.update(consts)
        in_maps.append(m)

    nc = build_nc()
    res = run_bass_kernel_spmd(nc, in_maps, core_ids=list(range(8)))

    outp = np.zeros((4, SEQ, D), dtype=np.float32)
    for core in range(8):
        bseq = core // 2
        blocks = ABLK if core % 2 == 0 else BBLK
        o = np.asarray(res.results[core]["outp"])
        for i, g in enumerate(blocks):
            outp[bseq, 512 * g:512 * (g + 1), :] = o[512 * i:512 * (i + 1), :]
    return outp


# version 03
# speedup vs baseline: 1.1020x; 1.1020x over previous
import numpy as np
import ml_dtypes
from contextlib import ExitStack

import concourse.bass as bass
import concourse.mybir as mybir
from concourse.bass_utils import run_bass_kernel_spmd

F32 = mybir.dt.float32
BF16 = mybir.dt.bfloat16
AF = mybir.ActivationFunctionType
ALU = mybir.AluOpType
AX = mybir.AxisListType.X

D = 2048
FF = 1408
NE = 8
FS = 2816
NH = 16
DH = 128
SEQ = 4096
TLOC = 2048
TC = 1024
NKC = D // 128
ISD = 0.08838834764831845
NEG = -1.0e9
EPS = 1.0e-5

ABLK = [0, 3, 4, 7]
BBLK = [1, 2, 5, 6]
SLOTS = [1024, 2048, 3072, 4096]

ENGS = ("sync", "gpsimd", "tensor", "scalar", "vector")


class Sem:
    def __init__(self, nc, name):
        self.name = name
        self.h = nc.alloc_semaphore(name=name)
        self.n = 0


def _flat(waits):
    out = []
    for t in waits:
        if t is None:
            continue
        if isinstance(t, list):
            out.extend(x for x in t if x is not None)
        else:
            out.append(t)
    return out


class Bld:
    def __init__(self, nc):
        self.nc = nc
        self.sQ = Sem(nc, "sQ")
        self.sG = Sem(nc, "sG")
        self.sPE = Sem(nc, "sPE")
        self.sACT = Sem(nc, "sACT")
        self.sDVE = Sem(nc, "sDVE")
        self.sGP = Sem(nc, "sGP")
        self.sCC = [Sem(nc, "sCC0"), Sem(nc, "sCC1")]
        self.ops = None
        self.hw = None
        self.stage = None
        self.stage_tok = [None, None]
        self.stage_i = 0

    def begin(self):
        self.ops = {e: [] for e in ENGS}
        self.hw = {e: {} for e in ENGS}

    def op(self, eng, fn, waits=(), inc=None, amt=1):
        ws = []
        for sem, val in _flat(waits):
            if val > self.hw[eng].get(sem.name, 0):
                self.hw[eng][sem.name] = val
                ws.append((sem, val))
        tok = None
        if inc is not None:
            inc.n += amt
            tok = (inc, inc.n)
        self.ops[eng].append((fn, ws, (inc, amt) if inc is not None else None))
        return tok

    def flush(self, name=None):
        self.op("sync", None, waits=[(self.sQ, self.sQ.n)])
        self.op("gpsimd", None, waits=[(self.sG, self.sG.n)])
        nc = self.nc
        with nc.Block(name) as blk:
            for eng in ENGS:
                lst = self.ops[eng]
                if not lst:
                    continue

                def make(lst=lst):
                    def emit(e):
                        for fn, ws, inc in lst:
                            for sem, val in ws:
                                e.wait_ge(sem.h, val)
                            if fn is None:
                                continue
                            ins = fn(e)
                            if inc is not None:
                                ins.then_inc(inc[0].h, inc[1])
                    return emit
                getattr(blk, eng)(make())
        self.ops = None

    def qdma(self, out, in_, waits=(), transpose=False):
        if transpose:
            return self.op("sync", lambda e: e.dma_start(out, in_, transpose=True),
                           waits, self.sQ, 16)
        return self.op("sync", lambda e: e.dma_start(out, in_), waits, self.sQ, 16)

    def gdma(self, out, in_, waits=()):
        return self.op("gpsimd", lambda e: e.dma_start(out, in_), waits, self.sG, 16)

    def mm(self, out, lhsT, rhs, start, stop, waits=(), inc=False):
        return self.op("tensor",
                       lambda e: e.matmul(out, lhsT, rhs, start=start, stop=stop),
                       waits, self.sPE if inc else None)

    def tr(self, out, in_, ident, waits=()):
        return self.op("tensor", lambda e: e.transpose(out, in_, ident),
                       waits, self.sPE)

    def act(self, fn, out, in_, waits=(), **kw):
        return self.op("scalar", lambda e: e.activation(out, in_, fn, **kw),
                       waits, self.sACT)

    def dve(self, meth, *args, waits=(), **kw):
        return self.op("vector", lambda e: getattr(e, meth)(*args, **kw),
                       waits, self.sDVE)

    def gpc(self, meth, *args, waits=(), **kw):
        return self.op("gpsimd", lambda e: getattr(e, meth)(*args, **kw),
                       waits, self.sGP)

    def create_wt_cast(self, src_f32, nb):
        return self.gdma(nb[:, :], src_f32)

    def create_wt(self, ct, nb, wt, rows, cols):
        st = None
        for k in range(cols // 128):
            j = self.stage_i % 2
            lt = self.qdma(self.stage[j][:, 0:rows], nb[0:rows, k * 128:(k + 1) * 128],
                           waits=[ct, self.stage_tok[j]], transpose=True)
            st = self.gdma(wt[k * 128:(k + 1) * 128, 0:rows], self.stage[j][:, 0:rows],
                           waits=[lt])
            self.stage_tok[j] = st
            self.stage_i += 1
        return st


def _ap(t, offset, dims):
    return bass.AP(t, offset, [list(d) for d in dims])


def build_nc(dbg=(), phases=("f1", "attn", "f2")):
    nc = bass.Bass("TRN2", target_bir_lowering=False)

    xin = nc.dram_tensor("xin", [TLOC, D], F32, kind="ExternalInput")
    w_q = nc.dram_tensor("w_q", [NH, DH, D], F32, kind="ExternalInput")
    w_k = nc.dram_tensor("w_k", [NH, DH, D], F32, kind="ExternalInput")
    w_v = nc.dram_tensor("w_v", [NH, DH, D], F32, kind="ExternalInput")
    w_o = nc.dram_tensor("w_o", [D, D], F32, kind="ExternalInput")
    w_r = nc.dram_tensor("w_r", [NE, D], F32, kind="ExternalInput")
    w_g = nc.dram_tensor("w_g", [NE, FF, D], F32, kind="ExternalInput")
    w_u = nc.dram_tensor("w_u", [NE, FF, D], F32, kind="ExternalInput")
    w_d = nc.dram_tensor("w_d", [NE, D, FF], F32, kind="ExternalInput")
    w_sg = nc.dram_tensor("w_sg", [FS, D], F32, kind="ExternalInput")
    w_su = nc.dram_tensor("w_su", [FS, D], F32, kind="ExternalInput")
    w_sd = nc.dram_tensor("w_sd", [D, FS], F32, kind="ExternalInput")
    identf = nc.dram_tensor("identf", [128, 128], F32, kind="ExternalInput")
    identb = nc.dram_tensor("identb", [128, 128], BF16, kind="ExternalInput")
    ocol_b = nc.dram_tensor("ocol_b", [128, 1], BF16, kind="ExternalInput")
    orow_b = nc.dram_tensor("orow_b", [1, 128], BF16, kind="ExternalInput")
    ocol_f = nc.dram_tensor("ocol_f", [128, 1], F32, kind="ExternalInput")
    orow_f = nc.dram_tensor("orow_f", [1, 128], F32, kind="ExternalInput")
    dmask_in = nc.dram_tensor("dmask_in", [4, 128, 8 * 512], BF16,
                              kind="ExternalInput")

    out = nc.dram_tensor("outp", [TLOC, D], F32, kind="ExternalOutput")

    def idram(name, shape, dt):
        kind = "ExternalOutput" if name in dbg else "Internal"
        return nc.dram_tensor(name, shape, dt, kind=kind)

    nb_q = idram("nb_q", [D, D], BF16); wt_q = idram("wt_q", [D, D], BF16)
    nb_k = idram("nb_k", [D, D], BF16); wt_k = idram("wt_k", [D, D], BF16)
    nb_v = idram("nb_v", [D, D], BF16); wt_v = idram("wt_v", [D, D], BF16)
    nb_o = idram("nb_o", [D, D], BF16); wt_o = idram("wt_o", [D, D], BF16)
    nb_r = idram("nb_r", [NE, D], BF16)
    nb_g = idram("nb_g", [NE, FF, D], BF16); wt_g = idram("wt_g", [NE, D, FF], BF16)
    nb_u = idram("nb_u", [NE, FF, D], BF16); wt_u = idram("wt_u", [NE, D, FF], BF16)
    nb_d = idram("nb_d", [NE, D, FF], BF16); wt_d = idram("wt_d", [NE, FF, D], BF16)
    nb_sg = idram("nb_sg", [FS, D], BF16); wt_sg = idram("wt_sg", [D, FS], BF16)
    nb_su = idram("nb_su", [FS, D], BF16); wt_su = idram("wt_su", [D, FS], BF16)
    nb_sd = idram("nb_sd", [D, FS], BF16); wt_sd = idram("wt_sd", [FS, D], BF16)
    x_mid = idram("x_mid", [D, TLOC], F32)
    x2 = idram("x2", [D, TLOC], F32)
    qt = idram("qt", [D, TLOC], BF16)
    agin = [idram(f"agin{c}", [4112, TC], BF16) for c in range(2)]
    agout = [idram(f"agout{c}", [2 * 4112, TC], BF16) for c in range(2)]
    wbounce = idram("wbounce", [NE, TC], BF16)

    b = Bld(nc)
    wt_tok = {}

    def _job_desc(nm):
        if nm == "sg":
            return (w_sg[:, :], nb_sg, wt_sg, FS, D)
        if nm == "su":
            return (w_su[:, :], nb_su, wt_su, FS, D)
        if nm == "sd":
            return (w_sd[:, :], nb_sd, wt_sd, D, FS)
        if nm == "q":
            return (w_q.ap().rearrange("h p d -> (h p) d"), nb_q, wt_q, D, D)
        if nm == "k":
            return (w_k.ap().rearrange("h p d -> (h p) d"), nb_k, wt_k, D, D)
        if nm == "v":
            return (w_v.ap().rearrange("h p d -> (h p) d"), nb_v, wt_v, D, D)
        if nm == "o":
            return (w_o[:, :], nb_o, wt_o, D, D)
        kind, e = nm[0], int(nm[1])
        src = {"g": w_g, "u": w_u, "d": w_d}[kind]
        nb = {"g": nb_g, "u": nb_u, "d": nb_d}[kind]
        wt = {"g": wt_g, "u": wt_u, "d": wt_d}[kind]
        rows, cols = (D, FF) if kind == "d" else (FF, D)
        return (src[e, :, :], nb[e, :, :], wt[e, :, :], rows, cols)

    def creation_jobs(names):
        descs = [(nm, _job_desc(nm)) for nm in names]
        if not descs:
            return
        casts = {}
        casts[descs[0][0]] = b.create_wt_cast(descs[0][1][0], descs[0][1][1])
        for i, (nm, (s, nb, wt, rows, cols)) in enumerate(descs):
            if i + 1 < len(descs):
                nm2, (s2, nb2, _w, _r, _c) = descs[i + 1]
                casts[nm2] = b.create_wt_cast(s2, nb2)
            wt_tok[nm] = b.create_wt(casts[nm], nb, wt, rows, cols)

    def kof_view(dst, n):
        return dst.ap().rearrange("p (kc f) -> p kc f", kc=n)

    def kof_src(wt2d, col0):
        return wt2d[:, col0:col0 + 128].rearrange("(kc p) f -> p kc f", p=128)

    with ExitStack() as top:
        ent = top.enter_context
        stage = [ent(nc.sbuf_tensor(f"stg{i}", [128, FS], BF16)) for i in range(2)]
        b.stage = stage
        c_idf = ent(nc.sbuf_tensor("c_idf", [128, 128], F32))
        c_idb = ent(nc.sbuf_tensor("c_idb", [128, 128], BF16))
        c_ocb = ent(nc.sbuf_tensor("c_ocb", [128, 1], BF16))
        c_orb = ent(nc.sbuf_tensor("c_orb", [1, 128], BF16))
        c_ocf = ent(nc.sbuf_tensor("c_ocf", [128, 1], F32))
        c_orf = ent(nc.sbuf_tensor("c_orf", [1, 128], F32))
        wrT = ent(nc.sbuf_tensor("wrT", [128, NKC * NE], BF16))

        b.begin()
        for dst, src in ((c_idf, identf), (c_idb, identb), (c_ocb, ocol_b),
                         (c_orb, orow_b), (c_ocf, ocol_f), (c_orf, orow_f)):
            b.qdma(dst[:, :], src[:, :])
        tcast = b.gdma(nb_r[:, :], w_r[:, :])
        with ExitStack() as bx:
            wr_sb = bx.enter_context(nc.sbuf_tensor("wr_sb", [NE, D], BF16))
            pb7 = bx.enter_context(nc.psum_tensor("pb7_0", [128, 1024], BF16))
            tl = b.qdma(wr_sb[:, :], nb_r[:, :], waits=[tcast])
            prev = None
            for k in range(NKC):
                tt = b.tr(pb7[:, 0:NE], wr_sb[0:NE, k * 128:(k + 1) * 128],
                          c_idb[0:NE, 0:NE], waits=[tl, prev])
                prev = b.dve("tensor_copy", wrT[:, k * NE:(k + 1) * NE],
                             pb7[:, 0:NE], waits=[tt])
            b.flush("b0")

        def ffn_block(c, first, src_fm, dst_fm, do_qkv, do_out, jobs):
            b.begin()
            war = {}

            def gate(key):
                return war.get(key, [])

            def setw(key, *toks):
                war[key] = [t for t in toks if t is not None]

            with ExitStack() as bx:
                tag = f"_{'f' if first else 's'}{c}"
                sbt = lambda nm, sh, dt: bx.enter_context(
                    nc.sbuf_tensor(nm + tag, sh, dt))
                pst = lambda nm, sh, dt: bx.enter_context(
                    nc.psum_tensor(nm + tag, sh, dt))
                acc = sbt("acc", [128, NKC * TC], F32)
                xn = sbt("xn", [128, NKC * TC], BF16)
                hs = sbt("hs", [128, 11 * TC], BF16)
                xsq = [sbt(f"xsq{i}", [128, 512], BF16) for i in range(2)]
                xm = [sbt(f"xm{i}", [128, 512], F32) for i in range(2)]
                mb = sbt("mb", [128, 512], F32)
                rb = sbt("rb", [128, 512], F32)
                kofg = [sbt(f"kofg{i}", [128, 2048], BF16) for i in range(2)]
                kofu = [sbt(f"kofu{i}", [128, 2048], BF16) for i in range(2)]
                kofd = [sbt(f"kofd{i}", [128, 11 * 128], BF16) for i in range(2)]
                wb = [sbt(f"wb{i}", [128, TC], BF16) for i in range(2)]
                sgt = [sbt(f"sgt{i}", [128, 512], BF16) for i in range(2)]
                h1 = [sbt(f"h1{i}", [128, 512], BF16) for i in range(2)]
                lg = sbt("lg", [128, 64], F32)
                m1 = sbt("m1", [128, 8], F32)
                m1n = sbt("m1n", [128, 8], F32)
                ee = sbt("ee", [128, 64], F32)
                es = sbt("es", [128, 8], F32)
                rs = sbt("rs", [128, 8], F32)
                prb = sbt("prb", [128, 64], F32)
                m1p = sbt("m1p", [128, 8], F32)
                msk = sbt("msk", [128, 64], F32)
                m2 = sbt("m2", [128, 8], F32)
                keep = sbt("keep", [128, 64], F32)
                ww = sbt("ww", [128, 64], F32)
                wtb = sbt("wtb", [128, 64], BF16)
                wtr = sbt("wtr", [NE, TC], BF16)
                wrow = sbt("wrow", [1, TC], BF16)
                rows = [sbt(f"row{i}", [1, 512], F32) for i in range(4)]
                rowb = [sbt(f"rowb{i}", [1, 512], BF16) for i in range(2)]
                xstage = [sbt("xstage0", [128, 2048], F32)] * 2
                if do_qkv:
                    kqv = [sbt(f"kqv{i}", [128, TC], BF16) for i in range(2)]
                    ksq = sbt("ksq", [128, TC], BF16)
                    knr = sbt("knr", [1, TC], BF16)
                    vtm = sbt("vtm", [128, TC], BF16)
                p0 = pst("p0", [128, 512], F32)
                p1 = pst("p1", [128, 512], F32)
                p2 = pst("p2", [128, 512], F32)
                p3 = pst("p3", [128, 512], F32)
                p4 = pst("p4", [128, 512], F32)
                p5 = pst("p5", [128, 512], F32)
                p6 = pst("p6", [128, 512], F32)
                p7 = pst("p7", [128, 1024], BF16)

                if first:
                    for tti in range(8):
                        j = 0
                        tl = b.qdma(xstage[j][:, :],
                                    xin[c * TC + tti * 128: c * TC + (tti + 1) * 128, :],
                                    waits=[gate(("xst", j))])
                        last = None
                        for k in range(NKC):
                            pj = (p5, p6)[k % 2]
                            ttr = b.tr(pj[:, 0:128],
                                       xstage[j][:, k * 128:(k + 1) * 128],
                                       c_idf[:, :], waits=[tl, gate(("tb", k % 2))])
                            tcp = b.dve("tensor_copy",
                                        acc[:, k * TC + tti * 128: k * TC + (tti + 1) * 128],
                                        pj[:, 0:128], waits=[ttr])
                            setw(("tb", k % 2), tcp)
                            last = tcp
                        setw(("xst", j), last)
                    tok_acc = last
                else:
                    tok_acc = b.qdma(
                        acc.ap().rearrange("p (kc t) -> p kc t", kc=NKC),
                        src_fm[:, c * TC:(c + 1) * TC]
                        .rearrange("(kc p) t -> p kc t", p=128))
                war["accready"] = [tok_acc]

                creation_jobs(jobs)

                def emit_ln():
                    last = None
                    for th in range(2):
                        o5 = th * 512
                        tsum = None
                        for k in range(NKC):
                            sl = acc[:, k * TC + o5: k * TC + o5 + 512]
                            tsum = b.mm(p0[0:1, :], c_ocf[:, 0:1], sl,
                                        start=(k == 0), stop=(k == NKC - 1),
                                        waits=[gate("accready"), gate("p0")],
                                        inc=(k == NKC - 1))
                        tsq = None
                        for k in range(NKC):
                            sl = acc[:, k * TC + o5: k * TC + o5 + 512]
                            ta = b.act(AF.Square, xsq[k % 2][:, :], sl,
                                       waits=[gate(("xsq", k % 2)), gate("accready")])
                            tsq = b.mm(p1[0:1, :], c_ocb[:, 0:1], xsq[k % 2][:, :],
                                       start=(k == 0), stop=(k == NKC - 1),
                                       waits=[ta, gate("p1")], inc=True)
                            setw(("xsq", k % 2), tsq)
                        t1 = b.act(AF.Copy, rows[0][0:1, :], p0[0:1, :], waits=[tsum])
                        setw("p0", t1)
                        t2 = b.act(AF.Copy, rows[1][0:1, :], p1[0:1, :], waits=[tsq])
                        setw("p1", t2)
                        t3 = b.dve("tensor_scalar", rows[2][0:1, :], rows[0][0:1, :],
                                   1.0 / D, None, ALU.mult, waits=[t1])
                        t4 = b.dve("tensor_tensor", rows[3][0:1, :], rows[2][0:1, :],
                                   rows[2][0:1, :], ALU.mult, waits=[t3])
                        t4 = b.dve("scalar_tensor_tensor", rows[3][0:1, :],
                                   rows[1][0:1, :], 1.0 / D, rows[3][0:1, :],
                                   ALU.mult, ALU.subtract, waits=[t2, t4])
                        t4 = b.dve("tensor_scalar", rows[3][0:1, :], rows[3][0:1, :],
                                   EPS, None, ALU.add, waits=[t4])
                        t5 = b.act(AF.Sqrt, rows[1][0:1, :], rows[3][0:1, :],
                                   waits=[t4])
                        t6 = b.dve("reciprocal", rows[3][0:1, :], rows[1][0:1, :],
                                   waits=[t5])
                        t7 = b.dve("tensor_copy", rowb[0][0:1, :], rows[2][0:1, :],
                                   waits=[t3])
                        t8 = b.dve("tensor_copy", rowb[1][0:1, :], rows[3][0:1, :],
                                   waits=[t6])
                        tmm = b.mm(p2[:, :], c_orb[0:1, :], rowb[0][0:1, :], True, True,
                                   waits=[t7, gate("p2")], inc=True)
                        tmb = b.act(AF.Copy, mb[:, :], p2[:, :], waits=[tmm, gate("mbf")])
                        setw("p2", tmb)
                        trm = b.mm(p3[:, :], c_orb[0:1, :], rowb[1][0:1, :], True, True,
                                   waits=[t8, gate("p3")], inc=True)
                        trb = b.act(AF.Copy, rb[:, :], p3[:, :], waits=[trm, gate("rbf")])
                        setw("p3", trb)
                        for k in range(NKC):
                            sl = acc[:, k * TC + o5: k * TC + o5 + 512]
                            j = k % 2
                            ts = b.dve("tensor_tensor", xm[j][:, :], sl,
                                       mb[:, :], ALU.subtract,
                                       waits=[tmb, gate(("xmf", j))])
                            tx = b.dve("tensor_tensor",
                                       xn[:, k * TC + o5: k * TC + o5 + 512],
                                       xm[j][:, :], rb[:, :],
                                       ALU.mult, waits=[trb])
                            setw(("xmf", j), tx)
                            last = tx
                        setw("mbf", last)
                        setw("rbf", last)
                    return last

                tok_xn = emit_ln()
                war["xnready"] = [tok_xn]

                for tti in range(8):
                    tlg = None
                    for k in range(NKC):
                        tlg = b.mm(p4[:, 0:8],
                                   xn[:, k * TC + tti * 128: k * TC + (tti + 1) * 128],
                                   wrT[:, k * NE:(k + 1) * NE],
                                   start=(k == 0), stop=(k == NKC - 1),
                                   waits=[gate("xnready"), gate("p4")],
                                   inc=(k == NKC - 1))
                    te = b.act(AF.Copy, lg[:, tti * 8:(tti + 1) * 8], p4[:, 0:8],
                               waits=[tlg])
                    setw("p4", te)
                tok_lg = war["p4"][0]
                tmx = None
                for tti in range(8):
                    tmx = b.dve("reduce_max", m1[:, tti:tti + 1],
                                lg[:, tti * 8:(tti + 1) * 8], AX, waits=[tok_lg])
                tneg = b.dve("tensor_scalar", m1n[:, :], m1[:, :], -1.0, None,
                             ALU.mult, waits=[tmx])
                texp = None
                for tti in range(8):
                    texp = b.act(AF.Exp, ee[:, tti * 8:(tti + 1) * 8],
                                 lg[:, tti * 8:(tti + 1) * 8],
                                 bias=m1n[:, tti:tti + 1], waits=[tneg])
                tes = None
                for tti in range(8):
                    tes = b.dve("reduce_sum", es[:, tti:tti + 1],
                                ee[:, tti * 8:(tti + 1) * 8], AX, waits=[texp])
                trs = b.dve("reciprocal", rs[:, :], es[:, :], waits=[tes])
                tpr = None
                for tti in range(8):
                    tpr = b.gpc("tensor_scalar", prb[:, tti * 8:(tti + 1) * 8],
                                ee[:, tti * 8:(tti + 1) * 8], rs[:, tti:tti + 1],
                                None, ALU.mult, waits=[trs])
                tm1p = []
                for tti in range(8):
                    tm1p.append(b.dve("reduce_max", m1p[:, tti:tti + 1],
                                      prb[:, tti * 8:(tti + 1) * 8], AX, waits=[tpr]))
                teq = None
                for tti in range(8):
                    teq = b.gpc("tensor_scalar", msk[:, tti * 8:(tti + 1) * 8],
                                prb[:, tti * 8:(tti + 1) * 8], m1p[:, tti:tti + 1],
                                None, ALU.is_equal, waits=[tm1p[-1]])
                tmsk = b.dve("scalar_tensor_tensor", msk[:, :], msk[:, :], NEG,
                             prb[:, :], ALU.mult, ALU.add, waits=[teq])
                tm2 = []
                for tti in range(8):
                    tm2.append(b.dve("reduce_max", m2[:, tti:tti + 1],
                                     msk[:, tti * 8:(tti + 1) * 8], AX, waits=[tmsk]))
                tke = None
                for tti in range(8):
                    tke = b.gpc("tensor_scalar", keep[:, tti * 8:(tti + 1) * 8],
                                prb[:, tti * 8:(tti + 1) * 8], m2[:, tti:tti + 1],
                                None, ALU.is_ge, waits=[tm2[-1]])
                tw = b.dve("tensor_tensor", ww[:, :], prb[:, :], keep[:, :],
                           ALU.mult, waits=[tke])
                tw = b.dve("tensor_scalar", wtb[:, :], ww[:, :], 0.5, None,
                           ALU.mult, waits=[tw])
                tprv = None
                for tti in range(8):
                    ttr = b.tr(p7[0:NE, 0:128], wtb[:, tti * 8:(tti + 1) * 8],
                               c_idb[:, :], waits=[tw, tprv])
                    tprv = b.dve("tensor_copy", wtr[0:NE, tti * 128:(tti + 1) * 128],
                                 p7[0:NE, 0:128], waits=[ttr])
                setw("p7", tprv)
                tok_wbounce = b.qdma(wbounce[:, :], wtr[:, :], waits=[tprv])

                def gate_up_down(srcs, wbt, tok_wb, half_scale):
                    gsrc, usrc, dsrc = srcs
                    hs_tok = [None, None]
                    tgm = tum = None
                    for fs in range(11):
                        jg = fs % 2
                        sa, wta = gsrc(fs)
                        tgl = b.qdma(kof_view(kofg[jg], NKC), sa,
                                     waits=[wta, gate(("kofg", jg))])
                        sb_, wtb_ = usrc(fs)
                        tul = b.qdma(kof_view(kofu[jg], NKC), sb_,
                                     waits=[wtb_, gate(("kofu", jg))])
                        for th in range(2):
                            o5 = th * 512
                            pg = (p0, p1)[th]
                            pu = (p2, p3)[th]
                            tgm = tum = None
                            for k in range(NKC):
                                rhs = xn[:, k * TC + o5: k * TC + o5 + 512]
                                tgm = b.mm(pg[:, :], kofg[jg][:, k * 128:(k + 1) * 128],
                                           rhs, start=(k == 0), stop=(k == NKC - 1),
                                           waits=[tgl, gate("xnready"),
                                                  gate("p0" if th == 0 else "p1")],
                                           inc=(k == NKC - 1))
                                tum = b.mm(pu[:, :], kofu[jg][:, k * 128:(k + 1) * 128],
                                           rhs, start=(k == 0), stop=(k == NKC - 1),
                                           waits=[tul,
                                                  gate("p2" if th == 0 else "p3")],
                                           inc=(k == NKC - 1))
                            js = th
                            tsl = b.act(AF.Silu, sgt[js][:, :], pg[:, :],
                                        waits=[tgm, gate(("sgt", js))])
                            setw("p0" if th == 0 else "p1", tsl)
                            hsl = hs[:, fs * TC + o5: fs * TC + o5 + 512]
                            if wbt is None:
                                th2 = b.dve("tensor_tensor", hsl, sgt[js][:, :],
                                            pu[:, :], ALU.mult, waits=[tsl, tum])
                            else:
                                th1 = b.dve("tensor_tensor", h1[js][:, :],
                                            sgt[js][:, :], pu[:, :], ALU.mult,
                                            waits=[tsl, tum, gate(("h1", js))])
                                th2 = b.dve("tensor_tensor", hsl, h1[js][:, :],
                                            wbt[:, o5:o5 + 512], ALU.mult,
                                            waits=[tok_wb])
                                setw(("h1", js), th2)
                            setw("p2" if th == 0 else "p3", th2)
                            setw(("sgt", js), th2)
                            hs_tok[th] = th2
                        setw(("kofg", jg), tgm)
                        setw(("kofu", jg), tum)
                    for dm in range(NKC):
                        jd = dm % 2
                        sd_, wtd_ = dsrc(dm)
                        tdl = b.qdma(kof_view(kofd[jd], 11), sd_,
                                     waits=[wtd_, gate(("kofd", jd))])
                        tdm = None
                        for th in range(2):
                            o5 = th * 512
                            pd = (p4, p5)[th]
                            tdm = None
                            for kf in range(11):
                                rhs = hs[:, kf * TC + o5: kf * TC + o5 + 512]
                                tdm = b.mm(pd[:, :], kofd[jd][:, kf * 128:(kf + 1) * 128],
                                           rhs, start=(kf == 0), stop=(kf == 10),
                                           waits=[tdl, hs_tok[th],
                                                  gate("p4" if th == 0 else "p5")],
                                           inc=(kf == 10))
                            asl = acc[:, dm * TC + o5: dm * TC + o5 + 512]
                            if half_scale:
                                tad = b.dve("scalar_tensor_tensor", asl, pd[:, :],
                                            0.5, asl, ALU.mult, ALU.add, waits=[tdm])
                            else:
                                tad = b.dve("tensor_tensor", asl, asl, pd[:, :],
                                            ALU.add, waits=[tdm])
                            setw("p4" if th == 0 else "p5", tad)
                            war["accw"] = [tad]
                        setw(("kofd", jd), tdm)

                for half in range(2):
                    gate_up_down(
                        (lambda fs, half=half: (kof_src(wt_sg, (half * 11 + fs) * 128),
                                                wt_tok.get("sg")),
                         lambda fs, half=half: (kof_src(wt_su, (half * 11 + fs) * 128),
                                                wt_tok.get("su")),
                         lambda dm, half=half: (
                             wt_sd[half * 1408:(half + 1) * 1408,
                                   dm * 128:(dm + 1) * 128]
                             .rearrange("(kc p) f -> p kc f", p=128),
                             wt_tok.get("sd"))),
                        None, None, True)

                for e in range(NE):
                    jw = e % 2
                    trl = b.qdma(wrow[0:1, :], wbounce[e:e + 1, :],
                                 waits=[tok_wbounce, gate("wrow")])
                    twb = None
                    for th in range(2):
                        o5 = th * 512
                        tbm = b.mm(p6[:, :], c_orb[0:1, :], wrow[0:1, o5:o5 + 512],
                                   True, True, waits=[trl, gate("p6")], inc=True)
                        twb = b.act(AF.Copy, wb[jw][:, o5:o5 + 512], p6[:, :],
                                    waits=[tbm, gate(("wb", jw))])
                        setw("p6", twb)
                    setw("wrow", twb)
                    setw(("wb", jw), twb)

                    gate_up_down(
                        (lambda fs, e=e: (wt_g[e, :, fs * 128:(fs + 1) * 128]
                                          .rearrange("(kc p) f -> p kc f", p=128),
                                          wt_tok.get(f"g{e}")),
                         lambda fs, e=e: (wt_u[e, :, fs * 128:(fs + 1) * 128]
                                          .rearrange("(kc p) f -> p kc f", p=128),
                                          wt_tok.get(f"u{e}")),
                         lambda dm, e=e: (wt_d[e, :, dm * 128:(dm + 1) * 128]
                                          .rearrange("(kc p) f -> p kc f", p=128),
                                          wt_tok.get(f"d{e}"))),
                        wb[jw], twb, False)
                    setw(("wb", jw), war["accw"][0])

                tok_accdone = war["accw"][0]
                war["accready"] = [tok_accdone]

                if dst_fm is not None:
                    b.qdma(dst_fm[:, c * TC:(c + 1) * TC]
                           .rearrange("(kc p) t -> p kc t", p=128),
                           acc.ap().rearrange("p (kc t) -> p kc t", kc=NKC),
                           waits=[tok_accdone])

                if do_out:
                    for tti in range(8):
                        j = 0
                        last = None
                        for k in range(NKC):
                            pj = (p0, p1)[k % 2]
                            ttr = b.tr(pj[:, 0:128],
                                       acc[:, k * TC + tti * 128: k * TC + (tti + 1) * 128],
                                       c_idf[:, :],
                                       waits=[tok_accdone,
                                              gate("p0" if k % 2 == 0 else "p1")])
                            tcp = b.dve("tensor_copy",
                                        xstage[j][:, k * 128:(k + 1) * 128],
                                        pj[:, 0:128], waits=[ttr, gate(("xst", j))])
                            setw("p0" if k % 2 == 0 else "p1", tcp)
                            last = tcp
                        tso = b.qdma(out[c * TC + tti * 128: c * TC + (tti + 1) * 128, :],
                                     xstage[j][:, :], waits=[last])
                        setw(("xst", j), tso)

                if do_qkv:
                    tok_xn2 = emit_ln()
                    war["xnready"] = [tok_xn2]
                    ctoks = []
                    for wtW, dest in ((wt_q, "q"), (wt_k, "k"), (wt_v, "v")):
                        for h in range(NH):
                            jg = h % 2
                            tkl = b.qdma(kof_view(kofg[jg], NKC),
                                         kof_src(wtW, h * 128),
                                         waits=[wt_tok.get(dest), gate(("kofg", jg))])
                            evt = tqm = None
                            for th in range(2):
                                o5 = th * 512
                                pq = (p0, p1)[th]
                                tqm = None
                                for k in range(NKC):
                                    rhs = xn[:, k * TC + o5: k * TC + o5 + 512]
                                    tqm = b.mm(pq[:, :],
                                               kofg[jg][:, k * 128:(k + 1) * 128],
                                               rhs, start=(k == 0),
                                               stop=(k == NKC - 1),
                                               waits=[tkl, gate("xnready"),
                                                      gate("p0" if th == 0 else "p1")],
                                               inc=(k == NKC - 1))
                                evt = b.act(AF.Copy, kqv[jg][:, o5:o5 + 512],
                                            pq[:, :],
                                            waits=[tqm, gate(("kqv", jg))])
                                setw("p0" if th == 0 else "p1", evt)
                            setw(("kofg", jg), tqm)
                            if dest == "q":
                                tq = b.qdma(qt[h * 128:(h + 1) * 128,
                                               c * TC:(c + 1) * TC],
                                            kqv[jg][:, :], waits=[evt])
                                setw(("kqv", jg), tq)
                            elif dest == "k":
                                tk = b.qdma(agin[c][h * 128:(h + 1) * 128, :],
                                            kqv[jg][:, :], waits=[evt])
                                ctoks.append(tk)
                                tsq2 = b.act(AF.Square, ksq[:, :], kqv[jg][:, :],
                                             waits=[evt, gate("ksq")])
                                setw(("kqv", jg), tk, tsq2)
                                tn = None
                                for th in range(2):
                                    o5 = th * 512
                                    tnm = b.mm(p4[0:1, :], c_ocb[:, 0:1],
                                               ksq[:, o5:o5 + 512], True, True,
                                               waits=[tsq2, gate("p4")], inc=True)
                                    tn = b.act(AF.Copy, knr[0:1, o5:o5 + 512],
                                               p4[0:1, :],
                                               waits=[tnm, gate("knr")])
                                    setw("p4", tn)
                                setw("ksq", tn)
                                tkn = b.qdma(agin[c][4096 + h:4097 + h, :],
                                             knr[0:1, :], waits=[tn])
                                setw("knr", tkn)
                                ctoks.append(tkn)
                            else:
                                tvs = None
                                for tti in range(8):
                                    ttr = b.tr(p7[:, 0:128],
                                               kqv[jg][:, tti * 128:(tti + 1) * 128],
                                               c_idb[:, :], waits=[evt, gate("p7")])
                                    tvs = b.dve("tensor_copy",
                                                vtm[:, tti * 128:(tti + 1) * 128],
                                                p7[:, 0:128],
                                                waits=[ttr, gate("vtm")])
                                    setw("p7", tvs)
                                setw(("kqv", jg), tvs)
                                tvst = b.qdma(agin[c][2048 + h * 128:
                                                      2048 + (h + 1) * 128, :],
                                              vtm[:, :], waits=[tvs])
                                setw("vtm", tvst)
                                ctoks.append(tvst)
                    b.op("gpsimd",
                         lambda e, c=c: e.collective_compute(
                             "AllGather", ALU.bypass,
                             replica_groups=[[0, 1], [2, 3], [4, 5], [6, 7]],
                             ins=[agin[c].ap().opt()],
                             outs=[agout[c].ap().opt()]),
                         waits=[(b.sQ, max(t[1] for t in ctoks))],
                         inc=b.sCC[c], amt=1)

            b.flush(f"ffn{tag}")

        def attn_block():
            b.begin()
            war = {}

            def gate(key):
                return war.get(key, [])

            def setw(key, *toks):
                war[key] = [t for t in toks if t is not None]

            with ExitStack() as bx:
                sbt = lambda nm, sh, dt: bx.enter_context(
                    nc.sbuf_tensor("a_" + nm, sh, dt))
                pst = lambda nm, sh, dt: bx.enter_context(
                    nc.psum_tensor("a_" + nm, sh, dt))
                qsb = sbt("qsb", [128, NH * 512], BF16)
                ktt = [sbt(f"ktt{i}", [128, SEQ], BF16) for i in range(2)]
                vtt = [sbt(f"vtt{i}", [128, SEQ], BF16) for i in range(2)]
                at = [sbt(f"at{i}", [128, 512], BF16) for i in range(2)]
                tmpS = [sbt(f"tmpS{i}", [128, 512], F32) for i in range(2)]
                dmsk = sbt("dmsk", [128, 8 * 512], BF16)
                zst = sbt("zst", [128, NH * 512], BF16)
                xms = sbt("xms", [128, NKC * 512], F32)
                rbs = [sbt(f"rbs{i}", [128, 512], F32) for i in range(2)]
                kofo = [sbt(f"kofo{i}", [128, 2048], BF16) for i in range(2)]
                qsq = [sbt(f"qsq{i}", [128, 512], BF16) for i in range(2)]
                nmca = sbt("nmca", [128, 16], F32)
                knb = sbt("knb", [16, TC], BF16)
                kmc = sbt("kmc", [16, 2], F32)
                kmx = sbt("kmx", [1, 32], F32)
                qm2 = sbt("qm2", [1, 16], F32)
                m2v = sbt("m2v", [1, 16], F32)
                msq = sbt("msq", [1, 16], F32)
                ngm = sbt("ngm", [1, 16], BF16)
                rcb = [sbt(f"rcb{i}", [1, 512], F32) for i in range(2)]
                p0 = pst("p0", [128, 512], F32)
                p1 = pst("p1", [128, 512], F32)
                p2 = pst("p2", [128, 512], F32)
                p3 = pst("p3", [128, 512], F32)
                p4 = pst("p4", [128, 512], F32)
                p5 = pst("p5", [128, 512], F32)
                p6 = pst("p6", [128, 512], F32)

                def gmap(g):
                    cAG = 0 if g < 4 else 1
                    r, c0 = {0: (0, 0), 3: (0, 512), 1: (1, 0), 2: (1, 512),
                             4: (0, 0), 7: (0, 512), 5: (1, 0), 6: (1, 512)}[g]
                    return cAG, r, c0

                tok_kmx = []
                for si, cAG in enumerate((0, 1)):
                    tmm = None
                    for r in range(2):
                        tl = b.qdma(knb[:, :],
                                    agout[cAG][r * 4112 + 4096: r * 4112 + 4112, :],
                                    waits=[(b.sCC[cAG], 1), gate("knb")])
                        tm_ = b.dve("reduce_max", kmc[:, r:r + 1], knb[:, :], AX,
                                    waits=[tl])
                        setw("knb", tm_)
                        tmm = tm_
                    tmm = b.dve("tensor_tensor", kmc[:, 0:1], kmc[:, 0:1],
                                kmc[:, 1:2], ALU.max, waits=[tmm])
                    ttr = b.tr(p6[0:1, 0:16], kmc[0:16, 0:1], c_idf[0:16, 0:16],
                               waits=[tmm, gate("p6")])
                    tcp = b.dve("tensor_copy", kmx[0:1, si * 16:(si + 1) * 16],
                                p6[0:1, 0:16], waits=[ttr])
                    setw("p6", tcp)
                    if si == 1:
                        tcp = b.dve("tensor_tensor", kmx[0:1, 16:32],
                                    kmx[0:1, 16:32], kmx[0:1, 0:16], ALU.max,
                                    waits=[tcp])
                    tok_kmx.append(tcp)

                for qc in range(4):
                    slot = SLOTS[qc]
                    nt = slot // 128
                    need = [(b.sCC[0], 1)] + ([(b.sCC[1], 1)] if slot > 2048 else [])
                    kslice = kmx[0:1, 0:16] if slot <= 2048 else kmx[0:1, 16:32]
                    tok_km = tok_kmx[0] if slot <= 2048 else tok_kmx[1]

                    tql = b.qdma(qsb.ap().rearrange("p (h t) -> p h t", h=NH),
                                 qt[:, qc * 512:(qc + 1) * 512]
                                 .rearrange("(h p) t -> p h t", p=128),
                                 waits=[gate("qsb")])
                    tdm = b.qdma(dmsk[:, :], dmask_in[qc, :, :],
                                 waits=[gate("dmsk")])
                    txl = b.qdma(xms.ap().rearrange("p (kc t) -> p kc t", kc=NKC),
                                 x_mid[:, qc * 512:(qc + 1) * 512]
                                 .rearrange("(kc p) t -> p kc t", p=128),
                                 waits=[gate("xms")])

                    tqmx = None
                    for h in range(NH):
                        j = h % 2
                        tq2 = b.act(AF.Square, qsq[j][:, :],
                                    qsb[:, h * 512:(h + 1) * 512],
                                    waits=[tql, gate(("qsq", j))])
                        tqn = b.mm(p5[0:1, :], c_ocb[:, 0:1], qsq[j][:, :],
                                   True, True, waits=[tq2, gate("p5")], inc=True)
                        setw(("qsq", j), tqn)
                        tqmx = b.dve("reduce_max", qm2[0:1, h:h + 1], p5[0:1, :],
                                     AX, waits=[tqn])
                        setw("p5", tqmx)
                    tv = b.dve("tensor_tensor", m2v[0:1, :], qm2[0:1, :], kslice,
                               ALU.mult, waits=[tok_km, tqmx])
                    tv = b.act(AF.Sqrt, msq[0:1, :], m2v[0:1, :], waits=[tv])
                    tok_ngm = b.dve("tensor_scalar", ngm[0:1, :], msq[0:1, :],
                                    -ISD, None, ALU.mult, waits=[tv])

                    tnm = b.mm(p6[:, 0:16], c_orb[0:1, :], ngm[0:1, :],
                               True, True, waits=[tok_ngm, gate("p6")], inc=True)
                    tnc = b.dve("tensor_copy", nmca[:, :], p6[:, 0:16],
                                waits=[tnm, gate("nmca")])
                    setw("p6", tnc)
                    for h in range(NH):
                        j = h % 2
                        ktoks, vtoks = [], []
                        for g in range(slot // 512):
                            cAG, r, c0 = gmap(g)
                            tk = b.qdma(ktt[j][:, g * 512:(g + 1) * 512],
                                        agout[cAG][r * 4112 + h * 128:
                                                   r * 4112 + (h + 1) * 128,
                                                   c0:c0 + 512],
                                        waits=[gate(("ktt", j))] + need)
                            ktoks.append(tk)
                            tvv = b.qdma(vtt[j][:, g * 512:(g + 1) * 512],
                                         agout[cAG][r * 4112 + 2048 + h * 128:
                                                    r * 4112 + 2048 + (h + 1) * 128,
                                                    c0:c0 + 512],
                                         waits=[gate(("vtt", j))] + need)
                            vtoks.append(tvv)
                        tdn = tav = None
                        pzb = (p2, p5)[h % 2]
                        pzk = "p2" if h % 2 == 0 else "p5"
                        texs = {}

                        def emit_s(ti):
                            pS = (p0, p1)[ti % 2]
                            bs = "p0" if ti % 2 == 0 else "p1"
                            tS = b.mm(pS[:, :], ktt[j][:, ti * 128:(ti + 1) * 128],
                                      qsb[:, h * 512:(h + 1) * 512], True, True,
                                      waits=[ktoks[ti // 4], tql, gate(bs)],
                                      inc=True)
                            ja = ti % 2
                            if ti >= nt - 8:
                                j8 = ti - (nt - 8)
                                tms = b.dve("tensor_tensor", tmpS[ja][:, :],
                                            pS[:, :],
                                            dmsk[:, j8 * 512:(j8 + 1) * 512],
                                            ALU.add,
                                            waits=[tS, tdm, gate(("tmpS", ja))])
                                setw(bs, tms)
                                tex = b.act(AF.Exp, at[ja][:, :], tmpS[ja][:, :],
                                            bias=nmca[:, h:h + 1], scale=ISD,
                                            waits=[tms, tnc, gate(("at", ja))])
                                setw(("tmpS", ja), tex)
                            else:
                                tex = b.act(AF.Exp, at[ja][:, :], pS[:, :],
                                            bias=nmca[:, h:h + 1], scale=ISD,
                                            waits=[tS, tnc, gate(("at", ja))])
                                setw(bs, tex)
                            texs[ti] = tex

                        def emit_da(ti):
                            nonlocal tdn, tav
                            ja = ti % 2
                            tdn = b.mm(p3[0:1, :], c_ocb[:, 0:1], at[ja][:, :],
                                       start=(ti == 0), stop=(ti == nt - 1),
                                       waits=[texs[ti], gate("p3")], inc=True)
                            tav = b.mm(pzb[:, :], vtt[j][:, ti * 128:(ti + 1) * 128],
                                       at[ja][:, :], start=(ti == 0),
                                       stop=(ti == nt - 1),
                                       waits=[vtoks[ti // 4], gate(pzk)],
                                       inc=True)
                            setw(("at", ja), tdn, tav)

                        emit_s(0)
                        for ti in range(1, nt):
                            emit_s(ti)
                            emit_da(ti - 1)
                        emit_da(nt - 1)
                        setw(("ktt", j), tav)
                        setw(("vtt", j), tav)
                        war.setdefault("nmca", []).append(tav)
                        jr = h % 2
                        trc = b.dve("reciprocal", rcb[jr][0:1, :], p3[0:1, :],
                                    waits=[tdn, gate(("rcb", jr))])
                        setw("p3", trc)
                        tbc = b.mm(p4[:, :], c_orf[0:1, :], rcb[jr][0:1, :],
                                   True, True, waits=[trc, gate("p4")], inc=True)
                        tre = b.act(AF.Copy, rbs[jr][:, :], p4[:, :],
                                    waits=[tbc, gate(("rbs", jr))])
                        setw("p4", tre)
                        setw(("rcb", jr), tre)
                        tz = b.dve("tensor_tensor", zst[:, h * 512:(h + 1) * 512],
                                   pzb[:, :], rbs[jr][:, :], ALU.mult,
                                   waits=[tre, tav, gate(("zs", h))])
                        setw(pzk, tz)
                        setw(("rbs", jr), tz)
                        war["zdone"] = [tz]
                    for dm in range(NKC):
                        jo = dm % 2
                        bo = "p5" if dm % 2 == 0 else "p6"
                        tol = b.qdma(kof_view(kofo[jo], NKC), kof_src(wt_o, dm * 128),
                                     waits=[wt_tok.get("o"), gate(("kofo", jo))])
                        two = None
                        for fk in range(NKC):
                            two = b.mm((p5, p6)[dm % 2][:, :],
                                       kofo[jo][:, fk * 128:(fk + 1) * 128],
                                       zst[:, fk * 512:(fk + 1) * 512],
                                       start=(fk == 0), stop=(fk == NKC - 1),
                                       waits=[tol, gate("zdone"), gate(bo)],
                                       inc=(fk == NKC - 1))
                        setw(("kofo", jo), two)
                        xsl = xms[:, dm * 512:(dm + 1) * 512]
                        tadd = b.dve("tensor_tensor", xsl, xsl,
                                     (p5, p6)[dm % 2][:, :], ALU.add,
                                     waits=[two, txl])
                        setw(bo, tadd)
                        tso = b.qdma(x2[dm * 128:(dm + 1) * 128,
                                        qc * 512:(qc + 1) * 512], xsl,
                                     waits=[tadd])
                        setw("xms", tso)
                    setw("qsb", tav)
                    setw("dmsk", tav)
                    for h in range(NH):
                        setw(("zs", h), two)

            b.flush("attn")

        do_kv = "attn" in phases
        ffn_block(0, True, None, x_mid, do_kv, False,
                  ["sg", "su", "sd"] + [f"{k}{e}" for e in range(NE) for k in "gud"]
                  + (["q", "k", "v"] if do_kv else []))
        ffn_block(1, True, None, x_mid, do_kv, False, ["o"] if do_kv else [])
        if "attn" in phases:
            attn_block()
        if "f2" in phases:
            ffn_block(0, False, x2, None, False, True, [])
            ffn_block(1, False, x2, None, False, True, [])

        b.begin()
        b.op("gpsimd", lambda e: e.nop(), waits=[(b.sQ, b.sQ.n), (b.sG, b.sG.n)])
        b.flush("fin")

    return nc



def _host_consts():
    return {
        "identf": np.eye(128, dtype=np.float32),
        "identb": np.eye(128, dtype=ml_dtypes.bfloat16),
        "ocol_b": np.ones((128, 1), dtype=ml_dtypes.bfloat16),
        "orow_b": np.ones((1, 128), dtype=ml_dtypes.bfloat16),
        "ocol_f": np.ones((128, 1), dtype=np.float32),
        "orow_f": np.ones((1, 128), dtype=np.float32),
    }


def _dmask_for(blocks):
    m = np.zeros((4, 128, 8, 512), dtype=np.float32)
    for qc in range(4):
        slot = SLOTS[qc]
        q0 = 512 * blocks[qc]
        for jj in range(8):
            ti = slot // 128 - 8 + jj
            kvpos = 128 * ti + np.arange(128)[:, None]
            qpos = q0 + np.arange(512)[None, :]
            m[qc, :, jj, :] = np.where(qpos >= kvpos, 0.0, NEG)
    return np.ascontiguousarray(m.reshape(4, 128, 8 * 512).astype(ml_dtypes.bfloat16))


def kernel(L0_attn_x, L0_attn_W_K, L0_attn_W_Q, L0_attn_W_V, L0_attn_W_O,
           L0_ffn_Wr, L0_ffn_Wg, L0_ffn_Wu, L0_ffn_Wd, L0_ffn_Wsg, L0_ffn_Wsu,
           L0_ffn_Wsd):
    x = np.ascontiguousarray(np.asarray(L0_attn_x, dtype=np.float32))
    ws = {
        "w_q": np.ascontiguousarray(np.asarray(L0_attn_W_Q, dtype=np.float32)),
        "w_k": np.ascontiguousarray(np.asarray(L0_attn_W_K, dtype=np.float32)),
        "w_v": np.ascontiguousarray(np.asarray(L0_attn_W_V, dtype=np.float32)),
        "w_o": np.ascontiguousarray(np.asarray(L0_attn_W_O, dtype=np.float32)),
        "w_r": np.ascontiguousarray(np.asarray(L0_ffn_Wr, dtype=np.float32)),
        "w_g": np.ascontiguousarray(np.asarray(L0_ffn_Wg, dtype=np.float32)),
        "w_u": np.ascontiguousarray(np.asarray(L0_ffn_Wu, dtype=np.float32)),
        "w_d": np.ascontiguousarray(np.asarray(L0_ffn_Wd, dtype=np.float32)),
        "w_sg": np.ascontiguousarray(np.asarray(L0_ffn_Wsg, dtype=np.float32)),
        "w_su": np.ascontiguousarray(np.asarray(L0_ffn_Wsu, dtype=np.float32)),
        "w_sd": np.ascontiguousarray(np.asarray(L0_ffn_Wsd, dtype=np.float32)),
    }
    consts = _host_consts()
    dmA = _dmask_for(ABLK)
    dmB = _dmask_for(BBLK)

    in_maps = []
    for core in range(8):
        bseq = core // 2
        blocks = ABLK if core % 2 == 0 else BBLK
        xs = np.concatenate([x[bseq, 512 * g:512 * (g + 1), :] for g in blocks],
                            axis=0)
        m = {"xin": np.ascontiguousarray(xs),
             "dmask_in": dmA if core % 2 == 0 else dmB}
        m.update(ws)
        m.update(consts)
        in_maps.append(m)

    nc = build_nc()
    res = run_bass_kernel_spmd(nc, in_maps, core_ids=list(range(8)))

    outp = np.zeros((4, SEQ, D), dtype=np.float32)
    for core in range(8):
        bseq = core // 2
        blocks = ABLK if core % 2 == 0 else BBLK
        o = np.asarray(res.results[core]["outp"])
        for i, g in enumerate(blocks):
            outp[bseq, 512 * g:512 * (g + 1), :] = o[512 * i:512 * (i + 1), :]
    return outp


# version 04
# speedup vs baseline: 1.2280x; 1.1144x over previous
import numpy as np
import ml_dtypes
from contextlib import ExitStack

import concourse.bass as bass
import concourse.mybir as mybir
from concourse.bass_utils import run_bass_kernel_spmd

F32 = mybir.dt.float32
BF16 = mybir.dt.bfloat16
AF = mybir.ActivationFunctionType
ALU = mybir.AluOpType
AX = mybir.AxisListType.X

D = 2048
FF = 1408
NE = 8
FS = 2816
NH = 16
DH = 128
SEQ = 4096
TLOC = 2048
TC = 1024
NKC = D // 128
ISD = 0.08838834764831845
NEG = -1.0e9
EPS = 1.0e-5

ABLK = [0, 3, 4, 7]
BBLK = [1, 2, 5, 6]
SLOTS = [1024, 2048, 3072, 4096]

ENGS = ("sync", "gpsimd", "tensor", "scalar", "vector")


class Sem:
    def __init__(self, nc, name):
        self.name = name
        self.h = nc.alloc_semaphore(name=name)
        self.n = 0


def _flat(waits):
    out = []
    for t in waits:
        if t is None:
            continue
        if isinstance(t, list):
            out.extend(x for x in t if x is not None)
        else:
            out.append(t)
    return out


class Bld:
    def __init__(self, nc):
        self.nc = nc
        self.sQ = Sem(nc, "sQ")
        self.sG = Sem(nc, "sG")
        self.sPE = Sem(nc, "sPE")
        self.sACT = Sem(nc, "sACT")
        self.sDVE = Sem(nc, "sDVE")
        self.sGP = Sem(nc, "sGP")
        self.sCC = [Sem(nc, "sCC0"), Sem(nc, "sCC1")]
        self.ops = None
        self.hw = None
        self.stage = None
        self.stage_tok = [None, None]
        self.stage_i = 0

    def begin(self):
        self.ops = {e: [] for e in ENGS}
        self.hw = {e: {} for e in ENGS}

    def op(self, eng, fn, waits=(), inc=None, amt=1):
        ws = []
        for sem, val in _flat(waits):
            if val > self.hw[eng].get(sem.name, 0):
                self.hw[eng][sem.name] = val
                ws.append((sem, val))
        tok = None
        if inc is not None:
            inc.n += amt
            tok = (inc, inc.n)
        self.ops[eng].append((fn, ws, (inc, amt) if inc is not None else None))
        return tok

    def flush(self, name=None):
        self.op("sync", None, waits=[(self.sQ, self.sQ.n)])
        self.op("gpsimd", None, waits=[(self.sG, self.sG.n)])
        nc = self.nc
        with nc.Block(name) as blk:
            for eng in ENGS:
                lst = self.ops[eng]
                if not lst:
                    continue

                def make(lst=lst):
                    def emit(e):
                        for fn, ws, inc in lst:
                            for sem, val in ws:
                                e.wait_ge(sem.h, val)
                            if fn is None:
                                continue
                            ins = fn(e)
                            if inc is not None:
                                ins.then_inc(inc[0].h, inc[1])
                    return emit
                getattr(blk, eng)(make())
        self.ops = None

    def qdma(self, out, in_, waits=(), transpose=False):
        if transpose:
            return self.op("sync", lambda e: e.dma_start(out, in_, transpose=True),
                           waits, self.sQ, 16)
        return self.op("sync", lambda e: e.dma_start(out, in_), waits, self.sQ, 16)

    def gdma(self, out, in_, waits=()):
        return self.op("gpsimd", lambda e: e.dma_start(out, in_), waits, self.sG, 16)

    def mm(self, out, lhsT, rhs, start, stop, waits=(), inc=False):
        return self.op("tensor",
                       lambda e: e.matmul(out, lhsT, rhs, start=start, stop=stop),
                       waits, self.sPE if inc else None)

    def tr(self, out, in_, ident, waits=()):
        return self.op("tensor", lambda e: e.transpose(out, in_, ident),
                       waits, self.sPE)

    def act(self, fn, out, in_, waits=(), **kw):
        return self.op("scalar", lambda e: e.activation(out, in_, fn, **kw),
                       waits, self.sACT)

    def dve(self, meth, *args, waits=(), **kw):
        return self.op("vector", lambda e: getattr(e, meth)(*args, **kw),
                       waits, self.sDVE)

    def gpc(self, meth, *args, waits=(), **kw):
        return self.op("gpsimd", lambda e: getattr(e, meth)(*args, **kw),
                       waits, self.sGP)

    def create_wt_cast(self, src_f32, nb):
        return self.gdma(nb[:, :], src_f32)


class Creator:

    def __init__(self, b, wt_tok):
        self.b = b
        self.wt_tok = wt_tok
        self.jobs = []

    def enqueue(self, descs):
        for nm, (s, nb, wt, rows, cols) in descs:
            ct = self.b.create_wt_cast(s, nb)
            self.jobs.append([nm, nb, wt, rows, cols, ct, 0])

    def step(self, n):
        b = self.b
        while n > 0 and self.jobs:
            nm, nb, wt, rows, cols, ct, k = self.jobs[0]
            j = b.stage_i % 2
            lt = b.qdma(b.stage[j][:, 0:rows], nb[0:rows, k * 128:(k + 1) * 128],
                        waits=[ct, b.stage_tok[j]], transpose=True)
            st = b.gdma(wt[k * 128:(k + 1) * 128, 0:rows], b.stage[j][:, 0:rows],
                        waits=[lt])
            b.stage_tok[j] = st
            b.stage_i += 1
            self.jobs[0][6] = k + 1
            if k + 1 == cols // 128:
                self.wt_tok[nm] = st
                self.jobs.pop(0)
            n -= 1

    def drain(self):
        self.step(10 ** 9)


def _ap(t, offset, dims):
    return bass.AP(t, offset, [list(d) for d in dims])


def build_nc(dbg=(), phases=("f1", "attn", "f2")):
    nc = bass.Bass("TRN2", target_bir_lowering=False)

    xin = nc.dram_tensor("xin", [TLOC, D], F32, kind="ExternalInput")
    w_q = nc.dram_tensor("w_q", [NH, DH, D], F32, kind="ExternalInput")
    w_k = nc.dram_tensor("w_k", [NH, DH, D], F32, kind="ExternalInput")
    w_v = nc.dram_tensor("w_v", [NH, DH, D], F32, kind="ExternalInput")
    w_o = nc.dram_tensor("w_o", [D, D], F32, kind="ExternalInput")
    w_r = nc.dram_tensor("w_r", [NE, D], F32, kind="ExternalInput")
    w_g = nc.dram_tensor("w_g", [NE, FF, D], F32, kind="ExternalInput")
    w_u = nc.dram_tensor("w_u", [NE, FF, D], F32, kind="ExternalInput")
    w_d = nc.dram_tensor("w_d", [NE, D, FF], F32, kind="ExternalInput")
    w_sg = nc.dram_tensor("w_sg", [FS, D], F32, kind="ExternalInput")
    w_su = nc.dram_tensor("w_su", [FS, D], F32, kind="ExternalInput")
    w_sd = nc.dram_tensor("w_sd", [D, FS], F32, kind="ExternalInput")
    identf = nc.dram_tensor("identf", [128, 128], F32, kind="ExternalInput")
    identb = nc.dram_tensor("identb", [128, 128], BF16, kind="ExternalInput")
    ocol_b = nc.dram_tensor("ocol_b", [128, 1], BF16, kind="ExternalInput")
    orow_b = nc.dram_tensor("orow_b", [1, 128], BF16, kind="ExternalInput")
    ocol_f = nc.dram_tensor("ocol_f", [128, 1], F32, kind="ExternalInput")
    orow_f = nc.dram_tensor("orow_f", [1, 128], F32, kind="ExternalInput")
    dmask_in = nc.dram_tensor("dmask_in", [4, 128, 8 * 512], BF16,
                              kind="ExternalInput")

    out = nc.dram_tensor("outp", [TLOC, D], F32, kind="ExternalOutput")

    def idram(name, shape, dt):
        kind = "ExternalOutput" if name in dbg else "Internal"
        return nc.dram_tensor(name, shape, dt, kind=kind)

    nb_q = idram("nb_q", [D, D], BF16); wt_q = idram("wt_q", [D, D], BF16)
    nb_k = idram("nb_k", [D, D], BF16); wt_k = idram("wt_k", [D, D], BF16)
    nb_v = idram("nb_v", [D, D], BF16); wt_v = idram("wt_v", [D, D], BF16)
    nb_o = idram("nb_o", [D, D], BF16); wt_o = idram("wt_o", [D, D], BF16)
    nb_r = idram("nb_r", [NE, D], BF16)
    nb_g = idram("nb_g", [NE, FF, D], BF16); wt_g = idram("wt_g", [NE, D, FF], BF16)
    nb_u = idram("nb_u", [NE, FF, D], BF16); wt_u = idram("wt_u", [NE, D, FF], BF16)
    nb_d = idram("nb_d", [NE, D, FF], BF16); wt_d = idram("wt_d", [NE, FF, D], BF16)
    nb_sg = idram("nb_sg", [FS, D], BF16); wt_sg = idram("wt_sg", [D, FS], BF16)
    nb_su = idram("nb_su", [FS, D], BF16); wt_su = idram("wt_su", [D, FS], BF16)
    nb_sd = idram("nb_sd", [D, FS], BF16); wt_sd = idram("wt_sd", [FS, D], BF16)
    x_mid = idram("x_mid", [D, TLOC], F32)
    x2 = idram("x2", [D, TLOC], F32)
    qt = idram("qt", [D, TLOC], BF16)
    agin = [idram(f"agin{c}", [4112, TC], BF16) for c in range(2)]
    agout = [idram(f"agout{c}", [2 * 4112, TC], BF16) for c in range(2)]
    wbounce = idram("wbounce", [NE, TC], BF16)

    b = Bld(nc)
    wt_tok = {}

    def _job_desc(nm):
        if nm == "sg":
            return (w_sg[:, :], nb_sg, wt_sg, FS, D)
        if nm == "su":
            return (w_su[:, :], nb_su, wt_su, FS, D)
        if nm == "sd":
            return (w_sd[:, :], nb_sd, wt_sd, D, FS)
        if nm == "q":
            return (w_q.ap().rearrange("h p d -> (h p) d"), nb_q, wt_q, D, D)
        if nm == "k":
            return (w_k.ap().rearrange("h p d -> (h p) d"), nb_k, wt_k, D, D)
        if nm == "v":
            return (w_v.ap().rearrange("h p d -> (h p) d"), nb_v, wt_v, D, D)
        if nm == "o":
            return (w_o[:, :], nb_o, wt_o, D, D)
        kind, e = nm[0], int(nm[1])
        src = {"g": w_g, "u": w_u, "d": w_d}[kind]
        nb = {"g": nb_g, "u": nb_u, "d": nb_d}[kind]
        wt = {"g": wt_g, "u": wt_u, "d": wt_d}[kind]
        rows, cols = (D, FF) if kind == "d" else (FF, D)
        return (src[e, :, :], nb[e, :, :], wt[e, :, :], rows, cols)

    creator = Creator(b, wt_tok)

    def creation_jobs(names):
        creator.enqueue([(nm, _job_desc(nm)) for nm in names])

    def kof_view(dst, n):
        return dst.ap().rearrange("p (kc f) -> p kc f", kc=n)

    def kof_src(wt2d, col0):
        return wt2d[:, col0:col0 + 128].rearrange("(kc p) f -> p kc f", p=128)

    with ExitStack() as top:
        ent = top.enter_context
        stage = [ent(nc.sbuf_tensor(f"stg{i}", [128, FS], BF16)) for i in range(2)]
        b.stage = stage
        c_idf = ent(nc.sbuf_tensor("c_idf", [128, 128], F32))
        c_idb = ent(nc.sbuf_tensor("c_idb", [128, 128], BF16))
        c_ocb = ent(nc.sbuf_tensor("c_ocb", [128, 1], BF16))
        c_orb = ent(nc.sbuf_tensor("c_orb", [1, 128], BF16))
        c_ocf = ent(nc.sbuf_tensor("c_ocf", [128, 1], F32))
        c_orf = ent(nc.sbuf_tensor("c_orf", [1, 128], F32))
        wrT = ent(nc.sbuf_tensor("wrT", [128, NKC * NE], BF16))

        b.begin()
        for dst, src in ((c_idf, identf), (c_idb, identb), (c_ocb, ocol_b),
                         (c_orb, orow_b), (c_ocf, ocol_f), (c_orf, orow_f)):
            b.qdma(dst[:, :], src[:, :])
        tcast = b.gdma(nb_r[:, :], w_r[:, :])
        with ExitStack() as bx:
            wr_sb = bx.enter_context(nc.sbuf_tensor("wr_sb", [NE, D], BF16))
            pb7 = bx.enter_context(nc.psum_tensor("pb7_0", [128, 1024], BF16))
            tl = b.qdma(wr_sb[:, :], nb_r[:, :], waits=[tcast])
            prev = None
            for k in range(NKC):
                tt = b.tr(pb7[:, 0:NE], wr_sb[0:NE, k * 128:(k + 1) * 128],
                          c_idb[0:NE, 0:NE], waits=[tl, prev])
                prev = b.dve("tensor_copy", wrT[:, k * NE:(k + 1) * NE],
                             pb7[:, 0:NE], waits=[tt])
            b.flush("b0")

        def ffn_block(c, first, src_fm, dst_fm, do_qkv, do_out, jobs):
            b.begin()
            war = {}

            def gate(key):
                return war.get(key, [])

            def setw(key, *toks):
                war[key] = [t for t in toks if t is not None]

            with ExitStack() as bx:
                tag = f"_{'f' if first else 's'}{c}"
                sbt = lambda nm, sh, dt: bx.enter_context(
                    nc.sbuf_tensor(nm + tag, sh, dt))
                pst = lambda nm, sh, dt: bx.enter_context(
                    nc.psum_tensor(nm + tag, sh, dt))
                acc = sbt("acc", [128, NKC * TC], F32)
                xn = sbt("xn", [128, NKC * TC], BF16)
                hs = sbt("hs", [128, 11 * TC], BF16)
                xsq = [sbt(f"xsq{i}", [128, 512], BF16) for i in range(2)]
                xm = [sbt(f"xm{i}", [128, 512], F32) for i in range(2)]
                mb = sbt("mb", [128, 512], F32)
                rb = sbt("rb", [128, 512], F32)
                kofg = [sbt(f"kofg{i}", [128, 2048], BF16) for i in range(2)]
                kofu = [sbt(f"kofu{i}", [128, 2048], BF16) for i in range(2)]
                kofd = [sbt(f"kofd{i}", [128, 11 * 128], BF16) for i in range(2)]
                wb = [sbt(f"wb{i}", [128, TC], BF16) for i in range(2)]
                sgt = [sbt(f"sgt{i}", [128, 512], BF16) for i in range(2)]
                h1 = [sbt(f"h1{i}", [128, 512], BF16) for i in range(2)]
                lg = sbt("lg", [128, 64], F32)
                m1 = sbt("m1", [128, 8], F32)
                m1n = sbt("m1n", [128, 8], F32)
                ee = sbt("ee", [128, 64], F32)
                es = sbt("es", [128, 8], F32)
                rs = sbt("rs", [128, 8], F32)
                prb = sbt("prb", [128, 64], F32)
                m1p = sbt("m1p", [128, 8], F32)
                msk = sbt("msk", [128, 64], F32)
                m2 = sbt("m2", [128, 8], F32)
                keep = sbt("keep", [128, 64], F32)
                ww = sbt("ww", [128, 64], F32)
                wtb = sbt("wtb", [128, 64], BF16)
                wtr = sbt("wtr", [NE, TC], BF16)
                wrow = sbt("wrow", [1, TC], BF16)
                rows = [sbt(f"row{i}", [1, 512], F32) for i in range(4)]
                rowb = [sbt(f"rowb{i}", [1, 512], BF16) for i in range(2)]
                xstage = [sbt("xstage0", [128, 2048], F32)] * 2
                if do_qkv:
                    kqv = [sbt(f"kqv{i}", [128, TC], BF16) for i in range(2)]
                    ksq = sbt("ksq", [128, TC], BF16)
                    knr = sbt("knr", [1, TC], BF16)
                    vtm = sbt("vtm", [128, TC], BF16)
                p0 = pst("p0", [128, 512], F32)
                p1 = pst("p1", [128, 512], F32)
                p2 = pst("p2", [128, 512], F32)
                p3 = pst("p3", [128, 512], F32)
                p4 = pst("p4", [128, 512], F32)
                p5 = pst("p5", [128, 512], F32)
                p6 = pst("p6", [128, 512], F32)
                p7 = pst("p7", [128, 1024], BF16)

                if first:
                    for tti in range(8):
                        j = 0
                        tl = b.qdma(xstage[j][:, :],
                                    xin[c * TC + tti * 128: c * TC + (tti + 1) * 128, :],
                                    waits=[gate(("xst", j))])
                        last = None
                        for k in range(NKC):
                            pj = (p5, p6)[k % 2]
                            ttr = b.tr(pj[:, 0:128],
                                       xstage[j][:, k * 128:(k + 1) * 128],
                                       c_idf[:, :], waits=[tl, gate(("tb", k % 2))])
                            tcp = b.dve("tensor_copy",
                                        acc[:, k * TC + tti * 128: k * TC + (tti + 1) * 128],
                                        pj[:, 0:128], waits=[ttr])
                            setw(("tb", k % 2), tcp)
                            last = tcp
                        setw(("xst", j), last)
                    tok_acc = last
                else:
                    tok_acc = b.qdma(
                        acc.ap().rearrange("p (kc t) -> p kc t", kc=NKC),
                        src_fm[:, c * TC:(c + 1) * TC]
                        .rearrange("(kc p) t -> p kc t", p=128))
                war["accready"] = [tok_acc]

                if jobs:
                    creation_jobs(["sg", "su", "sd", "g0", "u0", "d0"]
                                  if jobs == "c0" else ["o"])

                def emit_ln():
                    last = None
                    for th in range(2):
                        o5 = th * 512
                        tsum = None
                        for k in range(NKC):
                            sl = acc[:, k * TC + o5: k * TC + o5 + 512]
                            tsum = b.mm(p0[0:1, :], c_ocf[:, 0:1], sl,
                                        start=(k == 0), stop=(k == NKC - 1),
                                        waits=[gate("accready"), gate("p0")],
                                        inc=(k == NKC - 1))
                        tsq = None
                        for k in range(NKC):
                            creator.step(1)
                            sl = acc[:, k * TC + o5: k * TC + o5 + 512]
                            ta = b.act(AF.Square, xsq[k % 2][:, :], sl,
                                       waits=[gate(("xsq", k % 2)), gate("accready")])
                            tsq = b.mm(p1[0:1, :], c_ocb[:, 0:1], xsq[k % 2][:, :],
                                       start=(k == 0), stop=(k == NKC - 1),
                                       waits=[ta, gate("p1")], inc=True)
                            setw(("xsq", k % 2), tsq)
                        t1 = b.act(AF.Copy, rows[0][0:1, :], p0[0:1, :], waits=[tsum])
                        setw("p0", t1)
                        t2 = b.act(AF.Copy, rows[1][0:1, :], p1[0:1, :], waits=[tsq])
                        setw("p1", t2)
                        t3 = b.dve("tensor_scalar", rows[2][0:1, :], rows[0][0:1, :],
                                   1.0 / D, None, ALU.mult, waits=[t1])
                        t4 = b.dve("tensor_tensor", rows[3][0:1, :], rows[2][0:1, :],
                                   rows[2][0:1, :], ALU.mult, waits=[t3])
                        t4 = b.dve("scalar_tensor_tensor", rows[3][0:1, :],
                                   rows[1][0:1, :], 1.0 / D, rows[3][0:1, :],
                                   ALU.mult, ALU.subtract, waits=[t2, t4])
                        t4 = b.dve("tensor_scalar", rows[3][0:1, :], rows[3][0:1, :],
                                   EPS, None, ALU.add, waits=[t4])
                        t5 = b.act(AF.Sqrt, rows[1][0:1, :], rows[3][0:1, :],
                                   waits=[t4])
                        t6 = b.dve("reciprocal", rows[3][0:1, :], rows[1][0:1, :],
                                   waits=[t5])
                        t7 = b.dve("tensor_copy", rowb[0][0:1, :], rows[2][0:1, :],
                                   waits=[t3])
                        t8 = b.dve("tensor_copy", rowb[1][0:1, :], rows[3][0:1, :],
                                   waits=[t6])
                        tmm = b.mm(p2[:, :], c_orb[0:1, :], rowb[0][0:1, :], True, True,
                                   waits=[t7, gate("p2")], inc=True)
                        tmb = b.act(AF.Copy, mb[:, :], p2[:, :], waits=[tmm, gate("mbf")])
                        setw("p2", tmb)
                        trm = b.mm(p3[:, :], c_orb[0:1, :], rowb[1][0:1, :], True, True,
                                   waits=[t8, gate("p3")], inc=True)
                        trb = b.act(AF.Copy, rb[:, :], p3[:, :], waits=[trm, gate("rbf")])
                        setw("p3", trb)
                        for k in range(NKC):
                            sl = acc[:, k * TC + o5: k * TC + o5 + 512]
                            j = k % 2
                            ts = b.dve("tensor_tensor", xm[j][:, :], sl,
                                       mb[:, :], ALU.subtract,
                                       waits=[tmb, gate(("xmf", j))])
                            tx = b.dve("tensor_tensor",
                                       xn[:, k * TC + o5: k * TC + o5 + 512],
                                       xm[j][:, :], rb[:, :],
                                       ALU.mult, waits=[trb])
                            setw(("xmf", j), tx)
                            last = tx
                        setw("mbf", last)
                        setw("rbf", last)
                    return last

                tok_xn = emit_ln()
                war["xnready"] = [tok_xn]

                for tti in range(8):
                    tlg = None
                    for k in range(NKC):
                        tlg = b.mm(p4[:, 0:8],
                                   xn[:, k * TC + tti * 128: k * TC + (tti + 1) * 128],
                                   wrT[:, k * NE:(k + 1) * NE],
                                   start=(k == 0), stop=(k == NKC - 1),
                                   waits=[gate("xnready"), gate("p4")],
                                   inc=(k == NKC - 1))
                    te = b.act(AF.Copy, lg[:, tti * 8:(tti + 1) * 8], p4[:, 0:8],
                               waits=[tlg])
                    setw("p4", te)
                tok_lg = war["p4"][0]
                tmx = None
                for tti in range(8):
                    tmx = b.dve("reduce_max", m1[:, tti:tti + 1],
                                lg[:, tti * 8:(tti + 1) * 8], AX, waits=[tok_lg])
                tneg = b.dve("tensor_scalar", m1n[:, :], m1[:, :], -1.0, None,
                             ALU.mult, waits=[tmx])
                texp = None
                for tti in range(8):
                    texp = b.act(AF.Exp, ee[:, tti * 8:(tti + 1) * 8],
                                 lg[:, tti * 8:(tti + 1) * 8],
                                 bias=m1n[:, tti:tti + 1], waits=[tneg])
                tes = None
                for tti in range(8):
                    tes = b.dve("reduce_sum", es[:, tti:tti + 1],
                                ee[:, tti * 8:(tti + 1) * 8], AX, waits=[texp])
                trs = b.dve("reciprocal", rs[:, :], es[:, :], waits=[tes])
                tpr = None
                for tti in range(8):
                    tpr = b.gpc("tensor_scalar", prb[:, tti * 8:(tti + 1) * 8],
                                ee[:, tti * 8:(tti + 1) * 8], rs[:, tti:tti + 1],
                                None, ALU.mult, waits=[trs])
                tm1p = []
                for tti in range(8):
                    tm1p.append(b.dve("reduce_max", m1p[:, tti:tti + 1],
                                      prb[:, tti * 8:(tti + 1) * 8], AX, waits=[tpr]))
                teq = None
                for tti in range(8):
                    teq = b.gpc("tensor_scalar", msk[:, tti * 8:(tti + 1) * 8],
                                prb[:, tti * 8:(tti + 1) * 8], m1p[:, tti:tti + 1],
                                None, ALU.is_equal, waits=[tm1p[-1]])
                tmsk = b.dve("scalar_tensor_tensor", msk[:, :], msk[:, :], NEG,
                             prb[:, :], ALU.mult, ALU.add, waits=[teq])
                tm2 = []
                for tti in range(8):
                    tm2.append(b.dve("reduce_max", m2[:, tti:tti + 1],
                                     msk[:, tti * 8:(tti + 1) * 8], AX, waits=[tmsk]))
                tke = None
                for tti in range(8):
                    tke = b.gpc("tensor_scalar", keep[:, tti * 8:(tti + 1) * 8],
                                prb[:, tti * 8:(tti + 1) * 8], m2[:, tti:tti + 1],
                                None, ALU.is_ge, waits=[tm2[-1]])
                tw = b.dve("tensor_tensor", ww[:, :], prb[:, :], keep[:, :],
                           ALU.mult, waits=[tke])
                tw = b.dve("tensor_scalar", wtb[:, :], ww[:, :], 0.5, None,
                           ALU.mult, waits=[tw])
                tprv = None
                for tti in range(8):
                    ttr = b.tr(p7[0:NE, 0:128], wtb[:, tti * 8:(tti + 1) * 8],
                               c_idb[:, :], waits=[tw, tprv])
                    tprv = b.dve("tensor_copy", wtr[0:NE, tti * 128:(tti + 1) * 128],
                                 p7[0:NE, 0:128], waits=[ttr])
                setw("p7", tprv)
                tok_wbounce = b.qdma(wbounce[:, :], wtr[:, :], waits=[tprv])

                def gate_up_down(srcs, wbt, tok_wb, half_scale):
                    gsrc, usrc, dsrc = srcs
                    hs_tok = [None, None]
                    tgm = tum = None
                    for fs in range(11):
                        creator.step(2)
                        jg = fs % 2
                        sa, wta = gsrc(fs)
                        tgl = b.qdma(kof_view(kofg[jg], NKC), sa,
                                     waits=[wta, gate(("kofg", jg))])
                        sb_, wtb_ = usrc(fs)
                        tul = b.qdma(kof_view(kofu[jg], NKC), sb_,
                                     waits=[wtb_, gate(("kofu", jg))])
                        for th in range(2):
                            o5 = th * 512
                            pg = (p0, p1)[th]
                            pu = (p2, p3)[th]
                            tgm = tum = None
                            for k in range(NKC):
                                rhs = xn[:, k * TC + o5: k * TC + o5 + 512]
                                tgm = b.mm(pg[:, :], kofg[jg][:, k * 128:(k + 1) * 128],
                                           rhs, start=(k == 0), stop=(k == NKC - 1),
                                           waits=[tgl, gate("xnready"),
                                                  gate("p0" if th == 0 else "p1")],
                                           inc=(k == NKC - 1))
                                tum = b.mm(pu[:, :], kofu[jg][:, k * 128:(k + 1) * 128],
                                           rhs, start=(k == 0), stop=(k == NKC - 1),
                                           waits=[tul,
                                                  gate("p2" if th == 0 else "p3")],
                                           inc=(k == NKC - 1))
                            js = th
                            tsl = b.act(AF.Silu, sgt[js][:, :], pg[:, :],
                                        waits=[tgm, gate(("sgt", js))])
                            setw("p0" if th == 0 else "p1", tsl)
                            hsl = hs[:, fs * TC + o5: fs * TC + o5 + 512]
                            if wbt is None:
                                th2 = b.dve("tensor_tensor", hsl, sgt[js][:, :],
                                            pu[:, :], ALU.mult, waits=[tsl, tum])
                            else:
                                th1 = b.dve("tensor_tensor", h1[js][:, :],
                                            sgt[js][:, :], pu[:, :], ALU.mult,
                                            waits=[tsl, tum, gate(("h1", js))])
                                th2 = b.dve("tensor_tensor", hsl, h1[js][:, :],
                                            wbt[:, o5:o5 + 512], ALU.mult,
                                            waits=[tok_wb])
                                setw(("h1", js), th2)
                            setw("p2" if th == 0 else "p3", th2)
                            setw(("sgt", js), th2)
                            hs_tok[th] = th2
                        setw(("kofg", jg), tgm)
                        setw(("kofu", jg), tum)
                    for dm in range(NKC):
                        creator.step(2)
                        jd = dm % 2
                        sd_, wtd_ = dsrc(dm)
                        tdl = b.qdma(kof_view(kofd[jd], 11), sd_,
                                     waits=[wtd_, gate(("kofd", jd))])
                        tdm = None
                        for th in range(2):
                            o5 = th * 512
                            pd = (p4, p5)[th]
                            tdm = None
                            for kf in range(11):
                                rhs = hs[:, kf * TC + o5: kf * TC + o5 + 512]
                                tdm = b.mm(pd[:, :], kofd[jd][:, kf * 128:(kf + 1) * 128],
                                           rhs, start=(kf == 0), stop=(kf == 10),
                                           waits=[tdl, hs_tok[th],
                                                  gate("p4" if th == 0 else "p5")],
                                           inc=(kf == 10))
                            asl = acc[:, dm * TC + o5: dm * TC + o5 + 512]
                            if half_scale:
                                tad = b.dve("scalar_tensor_tensor", asl, pd[:, :],
                                            0.5, asl, ALU.mult, ALU.add, waits=[tdm])
                            else:
                                tad = b.dve("tensor_tensor", asl, asl, pd[:, :],
                                            ALU.add, waits=[tdm])
                            setw("p4" if th == 0 else "p5", tad)
                            war["accw"] = [tad]
                        setw(("kofd", jd), tdm)

                for half in range(2):
                    gate_up_down(
                        (lambda fs, half=half: (kof_src(wt_sg, (half * 11 + fs) * 128),
                                                wt_tok["sg"]),
                         lambda fs, half=half: (kof_src(wt_su, (half * 11 + fs) * 128),
                                                wt_tok["su"]),
                         lambda dm, half=half: (
                             wt_sd[half * 1408:(half + 1) * 1408,
                                   dm * 128:(dm + 1) * 128]
                             .rearrange("(kc p) f -> p kc f", p=128),
                             wt_tok["sd"])),
                        None, None, True)

                for e in range(NE):
                    if jobs == "c0":
                        if e < NE - 1:
                            creation_jobs([f"g{e + 1}", f"u{e + 1}", f"d{e + 1}"])
                        if e == 4:
                            creation_jobs(["q"])
                        if e == 5:
                            creation_jobs(["k"])
                        if e == 6:
                            creation_jobs(["v"])
                    jw = e % 2
                    trl = b.qdma(wrow[0:1, :], wbounce[e:e + 1, :],
                                 waits=[tok_wbounce, gate("wrow")])
                    twb = None
                    for th in range(2):
                        o5 = th * 512
                        tbm = b.mm(p6[:, :], c_orb[0:1, :], wrow[0:1, o5:o5 + 512],
                                   True, True, waits=[trl, gate("p6")], inc=True)
                        twb = b.act(AF.Copy, wb[jw][:, o5:o5 + 512], p6[:, :],
                                    waits=[tbm, gate(("wb", jw))])
                        setw("p6", twb)
                    setw("wrow", twb)
                    setw(("wb", jw), twb)

                    gate_up_down(
                        (lambda fs, e=e: (wt_g[e, :, fs * 128:(fs + 1) * 128]
                                          .rearrange("(kc p) f -> p kc f", p=128),
                                          wt_tok[f"g{e}"]),
                         lambda fs, e=e: (wt_u[e, :, fs * 128:(fs + 1) * 128]
                                          .rearrange("(kc p) f -> p kc f", p=128),
                                          wt_tok[f"u{e}"]),
                         lambda dm, e=e: (wt_d[e, :, dm * 128:(dm + 1) * 128]
                                          .rearrange("(kc p) f -> p kc f", p=128),
                                          wt_tok[f"d{e}"])),
                        wb[jw], twb, False)
                    setw(("wb", jw), war["accw"][0])

                tok_accdone = war["accw"][0]
                war["accready"] = [tok_accdone]

                if dst_fm is not None:
                    b.qdma(dst_fm[:, c * TC:(c + 1) * TC]
                           .rearrange("(kc p) t -> p kc t", p=128),
                           acc.ap().rearrange("p (kc t) -> p kc t", kc=NKC),
                           waits=[tok_accdone])

                if do_out:
                    for tti in range(8):
                        j = 0
                        last = None
                        for k in range(NKC):
                            pj = (p0, p1)[k % 2]
                            ttr = b.tr(pj[:, 0:128],
                                       acc[:, k * TC + tti * 128: k * TC + (tti + 1) * 128],
                                       c_idf[:, :],
                                       waits=[tok_accdone,
                                              gate("p0" if k % 2 == 0 else "p1")])
                            tcp = b.dve("tensor_copy",
                                        xstage[j][:, k * 128:(k + 1) * 128],
                                        pj[:, 0:128], waits=[ttr, gate(("xst", j))])
                            setw("p0" if k % 2 == 0 else "p1", tcp)
                            last = tcp
                        tso = b.qdma(out[c * TC + tti * 128: c * TC + (tti + 1) * 128, :],
                                     xstage[j][:, :], waits=[last])
                        setw(("xst", j), tso)

                if do_qkv:
                    tok_xn2 = emit_ln()
                    war["xnready"] = [tok_xn2]
                    ctoks = []
                    for wtW, dest in ((wt_q, "q"), (wt_k, "k"), (wt_v, "v")):
                        for h in range(NH):
                            creator.step(2)
                            jg = h % 2
                            tkl = b.qdma(kof_view(kofg[jg], NKC),
                                         kof_src(wtW, h * 128),
                                         waits=[wt_tok[dest], gate(("kofg", jg))])
                            evt = tqm = None
                            for th in range(2):
                                o5 = th * 512
                                pq = (p0, p1)[th]
                                tqm = None
                                for k in range(NKC):
                                    rhs = xn[:, k * TC + o5: k * TC + o5 + 512]
                                    tqm = b.mm(pq[:, :],
                                               kofg[jg][:, k * 128:(k + 1) * 128],
                                               rhs, start=(k == 0),
                                               stop=(k == NKC - 1),
                                               waits=[tkl, gate("xnready"),
                                                      gate("p0" if th == 0 else "p1")],
                                               inc=(k == NKC - 1))
                                evt = b.act(AF.Copy, kqv[jg][:, o5:o5 + 512],
                                            pq[:, :],
                                            waits=[tqm, gate(("kqv", jg))])
                                setw("p0" if th == 0 else "p1", evt)
                            setw(("kofg", jg), tqm)
                            if dest == "q":
                                tq = b.qdma(qt[h * 128:(h + 1) * 128,
                                               c * TC:(c + 1) * TC],
                                            kqv[jg][:, :], waits=[evt])
                                setw(("kqv", jg), tq)
                            elif dest == "k":
                                tk = b.qdma(agin[c][h * 128:(h + 1) * 128, :],
                                            kqv[jg][:, :], waits=[evt])
                                ctoks.append(tk)
                                tsq2 = b.act(AF.Square, ksq[:, :], kqv[jg][:, :],
                                             waits=[evt, gate("ksq")])
                                setw(("kqv", jg), tk, tsq2)
                                tn = None
                                for th in range(2):
                                    o5 = th * 512
                                    tnm = b.mm(p4[0:1, :], c_ocb[:, 0:1],
                                               ksq[:, o5:o5 + 512], True, True,
                                               waits=[tsq2, gate("p4")], inc=True)
                                    tn = b.act(AF.Copy, knr[0:1, o5:o5 + 512],
                                               p4[0:1, :],
                                               waits=[tnm, gate("knr")])
                                    setw("p4", tn)
                                setw("ksq", tn)
                                tkn = b.qdma(agin[c][4096 + h:4097 + h, :],
                                             knr[0:1, :], waits=[tn])
                                setw("knr", tkn)
                                ctoks.append(tkn)
                            else:
                                tvs = None
                                for tti in range(8):
                                    ttr = b.tr(p7[:, 0:128],
                                               kqv[jg][:, tti * 128:(tti + 1) * 128],
                                               c_idb[:, :], waits=[evt, gate("p7")])
                                    tvs = b.dve("tensor_copy",
                                                vtm[:, tti * 128:(tti + 1) * 128],
                                                p7[:, 0:128],
                                                waits=[ttr, gate("vtm")])
                                    setw("p7", tvs)
                                setw(("kqv", jg), tvs)
                                tvst = b.qdma(agin[c][2048 + h * 128:
                                                      2048 + (h + 1) * 128, :],
                                              vtm[:, :], waits=[tvs])
                                setw("vtm", tvst)
                                ctoks.append(tvst)
                    b.op("gpsimd",
                         lambda e, c=c: e.collective_compute(
                             "AllGather", ALU.bypass,
                             replica_groups=[[0, 1], [2, 3], [4, 5], [6, 7]],
                             ins=[agin[c].ap().opt()],
                             outs=[agout[c].ap().opt()]),
                         waits=[(b.sQ, max(t[1] for t in ctoks))],
                         inc=b.sCC[c], amt=1)

            creator.drain()
            b.flush(f"ffn{tag}")

        def attn_block():
            b.begin()
            war = {}

            def gate(key):
                return war.get(key, [])

            def setw(key, *toks):
                war[key] = [t for t in toks if t is not None]

            with ExitStack() as bx:
                sbt = lambda nm, sh, dt: bx.enter_context(
                    nc.sbuf_tensor("a_" + nm, sh, dt))
                pst = lambda nm, sh, dt: bx.enter_context(
                    nc.psum_tensor("a_" + nm, sh, dt))
                qsb = sbt("qsb", [128, NH * 512], BF16)
                ktt = [sbt(f"ktt{i}", [128, SEQ], BF16) for i in range(2)]
                vtt = [sbt(f"vtt{i}", [128, SEQ], BF16) for i in range(2)]
                at = [sbt(f"at{i}", [128, 512], BF16) for i in range(2)]
                tmpS = [sbt(f"tmpS{i}", [128, 512], F32) for i in range(2)]
                dmsk = sbt("dmsk", [128, 8 * 512], BF16)
                zst = sbt("zst", [128, NH * 512], BF16)
                xms = sbt("xms", [128, NKC * 512], F32)
                rbs = [sbt(f"rbs{i}", [128, 512], F32) for i in range(2)]
                kofo = [sbt(f"kofo{i}", [128, 2048], BF16) for i in range(2)]
                qsq = [sbt(f"qsq{i}", [128, 512], BF16) for i in range(2)]
                nmca = sbt("nmca", [128, 16], F32)
                knb = sbt("knb", [16, TC], BF16)
                kmc = sbt("kmc", [16, 2], F32)
                kmx = sbt("kmx", [1, 32], F32)
                qm2 = sbt("qm2", [1, 16], F32)
                m2v = sbt("m2v", [1, 16], F32)
                msq = sbt("msq", [1, 16], F32)
                ngm = sbt("ngm", [1, 16], BF16)
                rcb = [sbt(f"rcb{i}", [1, 512], F32) for i in range(2)]
                p0 = pst("p0", [128, 512], F32)
                p1 = pst("p1", [128, 512], F32)
                p2 = pst("p2", [128, 512], F32)
                p3 = pst("p3", [128, 512], F32)
                p4 = pst("p4", [128, 512], F32)
                p5 = pst("p5", [128, 512], F32)
                p6 = pst("p6", [128, 512], F32)

                def gmap(g):
                    cAG = 0 if g < 4 else 1
                    r, c0 = {0: (0, 0), 3: (0, 512), 1: (1, 0), 2: (1, 512),
                             4: (0, 0), 7: (0, 512), 5: (1, 0), 6: (1, 512)}[g]
                    return cAG, r, c0

                tok_kmx = []
                for si, cAG in enumerate((0, 1)):
                    tmm = None
                    for r in range(2):
                        tl = b.qdma(knb[:, :],
                                    agout[cAG][r * 4112 + 4096: r * 4112 + 4112, :],
                                    waits=[(b.sCC[cAG], 1), gate("knb")])
                        tm_ = b.dve("reduce_max", kmc[:, r:r + 1], knb[:, :], AX,
                                    waits=[tl])
                        setw("knb", tm_)
                        tmm = tm_
                    tmm = b.dve("tensor_tensor", kmc[:, 0:1], kmc[:, 0:1],
                                kmc[:, 1:2], ALU.max, waits=[tmm])
                    ttr = b.tr(p6[0:1, 0:16], kmc[0:16, 0:1], c_idf[0:16, 0:16],
                               waits=[tmm, gate("p6")])
                    tcp = b.dve("tensor_copy", kmx[0:1, si * 16:(si + 1) * 16],
                                p6[0:1, 0:16], waits=[ttr])
                    setw("p6", tcp)
                    if si == 1:
                        tcp = b.dve("tensor_tensor", kmx[0:1, 16:32],
                                    kmx[0:1, 16:32], kmx[0:1, 0:16], ALU.max,
                                    waits=[tcp])
                    tok_kmx.append(tcp)

                for qc in range(4):
                    slot = SLOTS[qc]
                    nt = slot // 128
                    need = [(b.sCC[0], 1)] + ([(b.sCC[1], 1)] if slot > 2048 else [])
                    kslice = kmx[0:1, 0:16] if slot <= 2048 else kmx[0:1, 16:32]
                    tok_km = tok_kmx[0] if slot <= 2048 else tok_kmx[1]

                    tql = b.qdma(qsb.ap().rearrange("p (h t) -> p h t", h=NH),
                                 qt[:, qc * 512:(qc + 1) * 512]
                                 .rearrange("(h p) t -> p h t", p=128),
                                 waits=[gate("qsb")])
                    tdm = b.qdma(dmsk[:, :], dmask_in[qc, :, :],
                                 waits=[gate("dmsk")])
                    txl = b.qdma(xms.ap().rearrange("p (kc t) -> p kc t", kc=NKC),
                                 x_mid[:, qc * 512:(qc + 1) * 512]
                                 .rearrange("(kc p) t -> p kc t", p=128),
                                 waits=[gate("xms")])

                    tqmx = None
                    for h in range(NH):
                        j = h % 2
                        tq2 = b.act(AF.Square, qsq[j][:, :],
                                    qsb[:, h * 512:(h + 1) * 512],
                                    waits=[tql, gate(("qsq", j))])
                        tqn = b.mm(p5[0:1, :], c_ocb[:, 0:1], qsq[j][:, :],
                                   True, True, waits=[tq2, gate("p5")], inc=True)
                        setw(("qsq", j), tqn)
                        tqmx = b.dve("reduce_max", qm2[0:1, h:h + 1], p5[0:1, :],
                                     AX, waits=[tqn])
                        setw("p5", tqmx)
                    tv = b.dve("tensor_tensor", m2v[0:1, :], qm2[0:1, :], kslice,
                               ALU.mult, waits=[tok_km, tqmx])
                    tv = b.act(AF.Sqrt, msq[0:1, :], m2v[0:1, :], waits=[tv])
                    tok_ngm = b.dve("tensor_scalar", ngm[0:1, :], msq[0:1, :],
                                    -ISD, None, ALU.mult, waits=[tv])

                    tnm = b.mm(p6[:, 0:16], c_orb[0:1, :], ngm[0:1, :],
                               True, True, waits=[tok_ngm, gate("p6")], inc=True)
                    tnc = b.dve("tensor_copy", nmca[:, :], p6[:, 0:16],
                                waits=[tnm, gate("nmca")])
                    setw("p6", tnc)
                    for h in range(NH):
                        j = h % 2
                        ktoks, vtoks = [], []
                        for g in range(slot // 512):
                            cAG, r, c0 = gmap(g)
                            tk = b.qdma(ktt[j][:, g * 512:(g + 1) * 512],
                                        agout[cAG][r * 4112 + h * 128:
                                                   r * 4112 + (h + 1) * 128,
                                                   c0:c0 + 512],
                                        waits=[gate(("ktt", j))] + need)
                            ktoks.append(tk)
                            tvv = b.qdma(vtt[j][:, g * 512:(g + 1) * 512],
                                         agout[cAG][r * 4112 + 2048 + h * 128:
                                                    r * 4112 + 2048 + (h + 1) * 128,
                                                    c0:c0 + 512],
                                         waits=[gate(("vtt", j))] + need)
                            vtoks.append(tvv)
                        tdn = tav = None
                        pzb = (p2, p5)[h % 2]
                        pzk = "p2" if h % 2 == 0 else "p5"
                        texs = {}

                        def emit_s(ti):
                            pS = (p0, p1)[ti % 2]
                            bs = "p0" if ti % 2 == 0 else "p1"
                            tS = b.mm(pS[:, :], ktt[j][:, ti * 128:(ti + 1) * 128],
                                      qsb[:, h * 512:(h + 1) * 512], True, True,
                                      waits=[ktoks[ti // 4], tql, gate(bs)],
                                      inc=True)
                            ja = ti % 2
                            if ti >= nt - 8:
                                j8 = ti - (nt - 8)
                                tms = b.dve("tensor_tensor", tmpS[ja][:, :],
                                            pS[:, :],
                                            dmsk[:, j8 * 512:(j8 + 1) * 512],
                                            ALU.add,
                                            waits=[tS, tdm, gate(("tmpS", ja))])
                                setw(bs, tms)
                                tex = b.act(AF.Exp, at[ja][:, :], tmpS[ja][:, :],
                                            bias=nmca[:, h:h + 1], scale=ISD,
                                            waits=[tms, tnc, gate(("at", ja))])
                                setw(("tmpS", ja), tex)
                            else:
                                tex = b.act(AF.Exp, at[ja][:, :], pS[:, :],
                                            bias=nmca[:, h:h + 1], scale=ISD,
                                            waits=[tS, tnc, gate(("at", ja))])
                                setw(bs, tex)
                            texs[ti] = tex

                        def emit_da(ti):
                            nonlocal tdn, tav
                            ja = ti % 2
                            tdn = b.mm(p3[0:1, :], c_ocb[:, 0:1], at[ja][:, :],
                                       start=(ti == 0), stop=(ti == nt - 1),
                                       waits=[texs[ti], gate("p3")], inc=True)
                            tav = b.mm(pzb[:, :], vtt[j][:, ti * 128:(ti + 1) * 128],
                                       at[ja][:, :], start=(ti == 0),
                                       stop=(ti == nt - 1),
                                       waits=[vtoks[ti // 4], gate(pzk)],
                                       inc=True)
                            setw(("at", ja), tdn, tav)

                        emit_s(0)
                        for ti in range(1, nt):
                            emit_s(ti)
                            emit_da(ti - 1)
                        emit_da(nt - 1)
                        setw(("ktt", j), tav)
                        setw(("vtt", j), tav)
                        war.setdefault("nmca", []).append(tav)
                        jr = h % 2
                        trc = b.dve("reciprocal", rcb[jr][0:1, :], p3[0:1, :],
                                    waits=[tdn, gate(("rcb", jr))])
                        setw("p3", trc)
                        tbc = b.mm(p4[:, :], c_orf[0:1, :], rcb[jr][0:1, :],
                                   True, True, waits=[trc, gate("p4")], inc=True)
                        tre = b.act(AF.Copy, rbs[jr][:, :], p4[:, :],
                                    waits=[tbc, gate(("rbs", jr))])
                        setw("p4", tre)
                        setw(("rcb", jr), tre)
                        tz = b.dve("tensor_tensor", zst[:, h * 512:(h + 1) * 512],
                                   pzb[:, :], rbs[jr][:, :], ALU.mult,
                                   waits=[tre, tav, gate(("zs", h))])
                        setw(pzk, tz)
                        setw(("rbs", jr), tz)
                        war["zdone"] = [tz]
                    for dm in range(NKC):
                        jo = dm % 2
                        bo = "p5" if dm % 2 == 0 else "p6"
                        tol = b.qdma(kof_view(kofo[jo], NKC), kof_src(wt_o, dm * 128),
                                     waits=[wt_tok["o"], gate(("kofo", jo))])
                        two = None
                        for fk in range(NKC):
                            two = b.mm((p5, p6)[dm % 2][:, :],
                                       kofo[jo][:, fk * 128:(fk + 1) * 128],
                                       zst[:, fk * 512:(fk + 1) * 512],
                                       start=(fk == 0), stop=(fk == NKC - 1),
                                       waits=[tol, gate("zdone"), gate(bo)],
                                       inc=(fk == NKC - 1))
                        setw(("kofo", jo), two)
                        xsl = xms[:, dm * 512:(dm + 1) * 512]
                        tadd = b.dve("tensor_tensor", xsl, xsl,
                                     (p5, p6)[dm % 2][:, :], ALU.add,
                                     waits=[two, txl])
                        setw(bo, tadd)
                        tso = b.qdma(x2[dm * 128:(dm + 1) * 128,
                                        qc * 512:(qc + 1) * 512], xsl,
                                     waits=[tadd])
                        setw("xms", tso)
                    setw("qsb", tav)
                    setw("dmsk", tav)
                    for h in range(NH):
                        setw(("zs", h), two)

            b.flush("attn")

        do_kv = "attn" in phases
        ffn_block(0, True, None, x_mid, do_kv, False, "c0")
        ffn_block(1, True, None, x_mid, do_kv, False, "c1" if do_kv else "")
        if "attn" in phases:
            attn_block()
        if "f2" in phases:
            ffn_block(0, False, x2, None, False, True, "")
            ffn_block(1, False, x2, None, False, True, "")

        b.begin()
        b.op("gpsimd", lambda e: e.nop(), waits=[(b.sQ, b.sQ.n), (b.sG, b.sG.n)])
        b.flush("fin")

    return nc



def _host_consts():
    return {
        "identf": np.eye(128, dtype=np.float32),
        "identb": np.eye(128, dtype=ml_dtypes.bfloat16),
        "ocol_b": np.ones((128, 1), dtype=ml_dtypes.bfloat16),
        "orow_b": np.ones((1, 128), dtype=ml_dtypes.bfloat16),
        "ocol_f": np.ones((128, 1), dtype=np.float32),
        "orow_f": np.ones((1, 128), dtype=np.float32),
    }


def _dmask_for(blocks):
    m = np.zeros((4, 128, 8, 512), dtype=np.float32)
    for qc in range(4):
        slot = SLOTS[qc]
        q0 = 512 * blocks[qc]
        for jj in range(8):
            ti = slot // 128 - 8 + jj
            kvpos = 128 * ti + np.arange(128)[:, None]
            qpos = q0 + np.arange(512)[None, :]
            m[qc, :, jj, :] = np.where(qpos >= kvpos, 0.0, NEG)
    return np.ascontiguousarray(m.reshape(4, 128, 8 * 512).astype(ml_dtypes.bfloat16))


def kernel(L0_attn_x, L0_attn_W_K, L0_attn_W_Q, L0_attn_W_V, L0_attn_W_O,
           L0_ffn_Wr, L0_ffn_Wg, L0_ffn_Wu, L0_ffn_Wd, L0_ffn_Wsg, L0_ffn_Wsu,
           L0_ffn_Wsd):
    x = np.ascontiguousarray(np.asarray(L0_attn_x, dtype=np.float32))
    ws = {
        "w_q": np.ascontiguousarray(np.asarray(L0_attn_W_Q, dtype=np.float32)),
        "w_k": np.ascontiguousarray(np.asarray(L0_attn_W_K, dtype=np.float32)),
        "w_v": np.ascontiguousarray(np.asarray(L0_attn_W_V, dtype=np.float32)),
        "w_o": np.ascontiguousarray(np.asarray(L0_attn_W_O, dtype=np.float32)),
        "w_r": np.ascontiguousarray(np.asarray(L0_ffn_Wr, dtype=np.float32)),
        "w_g": np.ascontiguousarray(np.asarray(L0_ffn_Wg, dtype=np.float32)),
        "w_u": np.ascontiguousarray(np.asarray(L0_ffn_Wu, dtype=np.float32)),
        "w_d": np.ascontiguousarray(np.asarray(L0_ffn_Wd, dtype=np.float32)),
        "w_sg": np.ascontiguousarray(np.asarray(L0_ffn_Wsg, dtype=np.float32)),
        "w_su": np.ascontiguousarray(np.asarray(L0_ffn_Wsu, dtype=np.float32)),
        "w_sd": np.ascontiguousarray(np.asarray(L0_ffn_Wsd, dtype=np.float32)),
    }
    consts = _host_consts()
    dmA = _dmask_for(ABLK)
    dmB = _dmask_for(BBLK)

    in_maps = []
    for core in range(8):
        bseq = core // 2
        blocks = ABLK if core % 2 == 0 else BBLK
        xs = np.concatenate([x[bseq, 512 * g:512 * (g + 1), :] for g in blocks],
                            axis=0)
        m = {"xin": np.ascontiguousarray(xs),
             "dmask_in": dmA if core % 2 == 0 else dmB}
        m.update(ws)
        m.update(consts)
        in_maps.append(m)

    nc = build_nc()
    res = run_bass_kernel_spmd(nc, in_maps, core_ids=list(range(8)))

    outp = np.zeros((4, SEQ, D), dtype=np.float32)
    for core in range(8):
        bseq = core // 2
        blocks = ABLK if core % 2 == 0 else BBLK
        o = np.asarray(res.results[core]["outp"])
        for i, g in enumerate(blocks):
            outp[bseq, 512 * g:512 * (g + 1), :] = o[512 * i:512 * (i + 1), :]
    return outp
